# version 15
# baseline (speedup 1.0000x reference)
from contextlib import ExitStack
import numpy as np

import concourse.bass as bass
import concourse.mybir as mybir
from concourse.bass_utils import run_bass_kernel_spmd

F32 = mybir.dt.float32
BF16 = mybir.dt.bfloat16
AF = mybir.ActivationFunctionType
OP = mybir.AluOpType
AX = mybir.AxisListType

B, L, D, N, NE, II = 4, 4096, 1024, 16, 16, 512
T = 2048
NDC = 8
TC = 512
NTC = T // TC
NTT = T // 128
TP3 = T + 4
RING = 3
CLAG = RING
BIG = -1e30
MARK = 1e30

C_NEG1, C_ONE, C_EPS, C_HALF = 0, 1, 2, 3


class S:
    def __init__(self, nc, ctx, name):
        self.h = ctx.enter_context(nc.semaphore(name))
        self.n = 0

    def bump(self, k=1):
        self.n += k
        return self.n


def build_nc(dbg=False):
    DBG2 = [dbg]
    nc = bass.Bass("TRN2", target_bir_lowering=False, detect_race_conditions=False)
    ops = {k: [] for k in ("sync", "tensor", "vector", "scalar", "gpsimd")}

    def E(eng, fn):
        ops[eng].append(fn)

    def W(eng, sem, val):
        if val > 0:
            ops[eng].append(lambda e, s=sem.h, v=val: e.wait_ge(s, v))

    def din(name, shape, dt=F32):
        return nc.dram_tensor(name, shape, dt, kind="ExternalInput").ap()

    x_tm = din("x_tm", [T, D])
    xp_tm = din("xp_tm", [T, D])
    x_fm = din("x_fm", [D, T])
    w33 = din("w33", [128, NDC * 33])
    p_w33 = din("p_w33", [128, NDC * 33])
    gwp = din("gwp", [128, NDC * 16])
    convw = din("convw", [128, 4 * NDC])
    p_convw = din("p_convw", [128, 4 * NDC])
    cb_pp = din("cb_pp", [128, NDC])
    p_cb_pp = din("p_cb_pp", [128, NDC])
    dtw_rep = din("dtw_rep", [128, D])
    dtb_rep = din("dtb_rep", [128, D])
    p_dtw_rep = din("p_dtw_rep", [128, D])
    p_dtb_rep = din("p_dtb_rep", [128, D])
    alog = din("alog", [128, NDC * N])
    p_alog = din("p_alog", [128, NDC * N])
    dp_pp = din("dp_pp", [128, NDC])
    ebias = din("ebias", [128, NE])
    ebr = din("ebr", [128, NE * NTT])
    hgate = din("hgate", [128, 1])
    sel33 = din("sel33", [33, 33 * 128])
    sel16 = din("sel16", [16, 16 * 128])
    iden = din("iden", [128, 128])
    csts = din("csts", [128, 8])
    gp_w = din("gp_w", [NE, 128, NDC * II])
    up_w = din("up_w", [NE, 128, NDC * II])
    dn_w = din("dn_w", [NE, 128, 4 * D])

    KI = "ExternalOutput" if dbg else "Internal"
    act_dram = nc.dram_tensor("act_dram", [NE, NTC, 128, 4 * TC], BF16, kind=KI).ap()
    dn_dram = nc.dram_tensor("dn_dram", [NE, 128, 4 * D], BF16, kind=KI).ap()
    attn_dram = nc.dram_tensor("attn_dram", [NDC, 128, T], F32, kind=KI).ap()
    xc_dram = nc.dram_tensor("xc_dram", [2, NDC, 128, T], BF16, kind=KI).ap()
    dl_dram = nc.dram_tensor("dl_dram", [2, NTT, 128, NDC * 128], F32, kind=KI).ap()
    y_out = nc.dram_tensor("y_fm", [D, T], F32, kind="ExternalOutput").ap()
    if dbg:
        dbg_xn = nc.dram_tensor("dbg_xn", [128, NDC * TP3], BF16, kind="ExternalOutput").ap()
        dbg_misc = nc.dram_tensor("dbg_misc", [128, 4096], F32, kind="ExternalOutput").ap()
        dbg_ct = nc.dram_tensor("dbg_ct", [16, T], BF16, kind="ExternalOutput").ap()
        dbg_bb = nc.dram_tensor("dbg_bb", [128, N * TC], BF16, kind="ExternalOutput").ap()
        dbg_cc = nc.dram_tensor("dbg_cc", [128, N * TC], BF16, kind="ExternalOutput").ap()
        dbg_m2 = nc.dram_tensor("dbg_m2", [128, 4096], F32, kind="ExternalOutput").ap()
        dbg_m3 = nc.dram_tensor("dbg_m3", [128, 2048], BF16, kind="ExternalOutput").ap()

    with ExitStack() as ctx:
        def sb(name, shape, dt=F32):
            return ctx.enter_context(nc.sbuf_tensor(name, shape, dt))

        b_w33 = sb("b_w33", [128, NDC * 33], BF16)
        b_pw33 = sb("b_pw33", [128, NDC * 33], BF16)
        b_gwp = sb("b_gwp", [128, NDC * 16], BF16)
        s_convw = sb("s_convw", [128, 4 * NDC])
        s_pconvw = sb("s_pconvw", [128, 4 * NDC])
        b_cdiag = sb("b_cdiag", [128, 4 * 128], BF16)
        s_cb = sb("s_cb", [128, NDC])
        s_pcb = sb("s_pcb", [128, NDC])
        arf1 = sb("arf1", [128, 4096])
        s_dtwr = arf1[:, 0:1024]
        s_dtbr = arf1[:, 1024:2048]
        s_pdtwr = arf1[:, 2048:3072]
        s_pdtbr = arf1[:, 3072:4096]
        s_app = sb("s_app", [128, NDC * N])
        s_papp = sb("s_papp", [128, NDC * N])
        s_dp = sb("s_dp", [128, NDC])
        s_eb = sb("s_eb", [128, NE])
        b_sel33 = sb("b_sel33", [33, 33 * 128], BF16)
        b_sel16 = sb("b_sel16", [16, 16 * 128], BF16)
        b_iden = sb("b_iden", [128, 128], BF16)
        s_cst = sb("s_cst", [128, 8])
        s_hg = sb("s_hg", [128, 1])

        s_stat = sb("s_stat", [128, 16])
        s_zc = sb("s_zc", [128, 2])

        xn = sb("xn", [128, NDC * TP3], BF16)
        b_pjt = sb("b_pjt", [33, T], BF16)
        b_ppjt = sb("b_ppjt", [33, T], BF16)
        cT = sb("cT", [16, T], BF16)
        s_rt = sb("s_rt", [128, 112])
        s_cmb = sb("s_cmb", [128, NE])
        b_xc = sb("b_xc", [128, NDC * TC], BF16)

        b_B = sb("b_B", [128, N * TC], BF16)
        b_C = sb("b_C", [128, N * TC], BF16)
        TCX = TC + 4
        s_abar = sb("s_abar", [128, RING * TCX])
        s_wr = sb("s_wr", [128, RING * TCX])
        s_u = sb("s_u", [128, 2 * TC])
        s_dl = sb("s_dl", [128, 2 * TC])
        s_xnt = s_dl[:, 0:1024]
        b_xcl = sb("b_xcl", [128, 2 * TC], BF16)
        s_h8 = sb("s_h8", [128, 8 * TCX])
        s_xsq = s_wr[:, 0:1024]
        s_dltm = s_abar[:, 0:1024]
        s_dlfm = s_h8[:, 0:1024]
        s_pjf = s_abar[:, 1024:1090]
        s_carry = sb("s_carry", [128, 128])

        arf2 = sb("arf2", [128, 1536])
        s_xin = arf2[:, 0:1024]
        s_silu = arf2[:, 0:1024]
        s_m1 = arf2[:, 1024:1536]
        s_stage = sb("s_stage", [128, 2 * 1024])
        barena = sb("barena", [128, NDC * TP3], BF16)
        xnp_s = barena[:, :]
        b_gp0 = barena[:, 0 * 4096:1 * 4096]
        b_gp1 = barena[:, 1 * 4096:2 * 4096]
        b_up0 = barena[:, 2 * 4096:3 * 4096]
        b_up1 = barena[:, 3 * 4096:4 * 4096]
        b_actin = barena[:, 0:2 * 4 * TC]
        b_dnin = barena[:, 2 * 4 * TC:2 * 4 * TC + 2 * 4096]
        b_dnc = sb("b_dnc", [128, 1024], BF16)
        b_act = sb("b_act", [128, 4 * TC], BF16)

        s_y = arf1[:, 1024:1536]
        s_fev = arf1[:, 0:1024]
        s_ati = arf1[:, 1024:2048]
        s_xi = arf1[:, 2048:3072]
        s_o1 = arf1[:, 3072:4096]

        psF = ctx.enter_context(nc.psum_tensor("psF", [128, 8 * 512], F32))

        def FB(i, w=512):
            return psF[:, i * 512: i * 512 + w]

        sprc = s_h8[:, 4048:4112]

        def SP(n=1):
            for _ in range(n):
                E("vector", lambda e: e.tensor_scalar_mul(sprc, sprc, 1.0))

        dSm = S(nc, ctx, "dSm")
        dX = S(nc, ctx, "dX")
        dW = [S(nc, ctx, "dW0"), S(nc, ctx, "dW1")]
        dA = S(nc, ctx, "dA")
        dAI = [S(nc, ctx, "dAI0"), S(nc, ctx, "dAI1")]
        dDN = S(nc, ctx, "dDN")
        dDNI = [S(nc, ctx, "dDNI0"), S(nc, ctx, "dDNI1")]
        dAT = S(nc, ctx, "dAT")
        dATI = [S(nc, ctx, "dATI0"), S(nc, ctx, "dATI1")]
        dSpX = S(nc, ctx, "dSpX")
        dSpD = S(nc, ctx, "dSpD")
        dSlD = [S(nc, ctx, "dSlD0"), S(nc, ctx, "dSlD1")]
        dSlX = [S(nc, ctx, "dSlX0"), S(nc, ctx, "dSlX1")]
        dOut = S(nc, ctx, "dOut")
        tM = S(nc, ctx, "tM")
        vM = S(nc, ctx, "vM")
        sM = S(nc, ctx, "sM")
        vScan = S(nc, ctx, "vScan")
        sAb = S(nc, ctx, "sAb")
        sCp = S(nc, ctx, "sCp")
        gW = S(nc, ctx, "gW")
        vG = S(nc, ctx, "vG")
        sFr = S(nc, ctx, "sFr")

        s_iden_f = s_h8[:, 1024:1152]
        for dst, src in [(s_convw, convw), (s_pconvw, p_convw), (s_cb, cb_pp),
                         (s_pcb, p_cb_pp), (s_dtwr, dtw_rep), (s_dtbr, dtb_rep),
                         (s_pdtwr, p_dtw_rep), (s_pdtbr, p_dtb_rep),
                         (s_dp, dp_pp), (s_eb, ebias), (s_cst, csts),
                         (s_hg, hgate), (s_app, alog), (s_papp, p_alog),
                         (s_iden_f, iden), (s_h8[:, 3792:4048], ebr)]:
            dSm.bump(16)
            E("sync", lambda e, d=dst, s=src: e.dma_start(out=d[:, :], in_=s).then_inc(dSm.h, 16))
        SMALLS = DSM_ALL = dSm.n

        stage_k = [0]

        def stage_load(src_ap, pp, ww, cast_out_ap, inc_sems=()):
            k = stage_k[0]
            stage_k[0] += 1
            sl = k % 2
            W("sync", sFr, k - 1)
            dW[sl].bump(16)
            E("sync", lambda e, s=src_ap, sl=sl, p=pp, w=ww:
              e.dma_start(out=s_stage[:p, sl * 1024: sl * 1024 + w], in_=s).then_inc(dW[sl].h, 16))
            W("scalar", dW[sl], dW[sl].n)
            sFr.bump()
            E("scalar", lambda e, d=cast_out_ap, sl=sl, p=pp, w=ww, inc=tuple(inc_sems):
              e.copy(out=d, in_=s_stage[:p, sl * 1024: sl * 1024 + w]).then_inc(sFr.h, 1))

        for dst, src, pp, ftot in [(b_w33, w33, 128, NDC * 33), (b_pw33, p_w33, 128, NDC * 33),
                                   (b_gwp, gwp, 128, NDC * 16),
                                   (b_sel33, sel33, 33, 33 * 128), (b_sel16, sel16, 16, 16 * 128),
                                   (b_iden, iden, 128, 128)]:
            off = 0
            while off < ftot:
                w = min(1024, ftot - off)
                stage_load(src[:, off:off + w], pp, w, dst[:pp, off:off + w])
                off += w
        CASTS = sFr.n

        W("scalar", dSm, DSM_ALL)
        E("scalar", lambda e: e.activation(out=s_app[:, :], in_=s_app[:, :], func=AF.Exp))
        sM.bump(1)
        E("scalar", lambda e: e.activation(out=s_papp[:, :], in_=s_papp[:, :], func=AF.Exp).then_inc(sM.h, 1))
        APPX = sM.n
        W("vector", sM, APPX)
        E("vector", lambda e: e.tensor_scalar_mul(s_app[:, :], s_app[:, :], -1.0))
        E("vector", lambda e: e.tensor_scalar_mul(s_papp[:, :], s_papp[:, :], -1.0))
        E("vector", lambda e: e.memzero(s_stat[:, :]))
        E("vector", lambda e: e.memzero(s_rt[:, :]))
        for sl in range(RING):
            E("vector", lambda e, sl=sl: e.memzero(s_abar[:, sl * TCX:sl * TCX + 2]))
            E("vector", lambda e, sl=sl: e.memzero(s_wr[:, sl * TCX + 1 + TC:(sl + 1) * TCX]))
            E("vector", lambda e, sl=sl: e.memzero(s_abar[:, sl * TCX + 1 + TC:(sl + 1) * TCX]))
        E("vector", lambda e: e.memzero(sprc[:, :]))
        vM.bump()
        E("vector", lambda e: e.memzero(s_carry[:, :]).then_inc(vM.h, 1))
        APPN = vM.n



        ln_last_v = [0]
        ln_tr_t = [0]

        def emit_ln(src_dram, dst_fm):
            for tt in range(NTT):
                sl = 0
                W("sync", vM, ln_last_v[0])
                W("sync", tM, ln_tr_t[0])
                dX.bump(16)
                E("sync", lambda e, s=src_dram, tt=tt:
                  e.dma_start(out=s_xin[:, 0:1024],
                              in_=s[tt * 128:(tt + 1) * 128, :]).then_inc(dX.h, 16))
                xv = dX.n
                W("vector", dX, xv)
                W("scalar", dX, xv)
                sM.bump()
                E("scalar", lambda e:
                  e.activation(out=s_xsq[:, :], in_=s_xin[:, 0:1024],
                               func=AF.Square).then_inc(sM.h, 1))
                sqv = sM.n
                E("vector", lambda e:
                  e.reduce_sum(s_stat[:, 0:1], s_xin[:, 0:1024], axis=AX.X))
                W("vector", sM, sqv)
                E("vector", lambda e: e.reduce_sum(s_stat[:, 1:2], s_xsq[:, :], axis=AX.X))
                E("vector", lambda e: e.tensor_scalar_mul(s_stat[:, 2:3], s_stat[:, 0:1], 1.0 / D))
                E("vector", lambda e: e.tensor_scalar_mul(s_stat[:, 3:4], s_stat[:, 1:2], 1.0 / D))
                E("vector", lambda e: e.tensor_mul(s_stat[:, 4:5], s_stat[:, 2:3], s_stat[:, 2:3]))
                SP()
                vM.bump()
                E("vector", lambda e: e.tensor_sub(s_stat[:, 5:6], s_stat[:, 3:4],
                                                   s_stat[:, 4:5]).then_inc(vM.h, 1))
                vv = vM.n
                W("scalar", vM, vv)
                sM.bump()
                E("scalar", lambda e: e.activation(out=s_stat[:, 6:7], in_=s_stat[:, 5:6],
                                                   func=AF.Sqrt,
                                                   bias=s_cst[:, C_EPS:C_EPS + 1]).then_inc(sM.h, 1))
                sv = sM.n
                W("vector", sM, sv)
                E("vector", lambda e: e.reciprocal(s_stat[:, 7:8], s_stat[:, 6:7]))
                SP()
                W("vector", tM, ln_tr_t[0])
                vM.bump()
                E("vector", lambda e:
                  e.tensor_scalar(out=s_xnt[:, :], in0=s_xin[:, 0:1024],
                                  scalar1=s_stat[:, 2:3], scalar2=s_stat[:, 7:8],
                                  op0=OP.subtract, op1=OP.mult).then_inc(vM.h, 1))
                vv = vM.n
                ln_last_v[0] = vv
                W("tensor", vM, vv)
                for dc in range(NDC):
                    tM.bump()
                    E("tensor", lambda e, dc=dc:
                      e.transpose(out=psF[:, 5 * 512 + dc * 128: 5 * 512 + (dc + 1) * 128],
                                  in_=s_xnt[:, dc * 128:(dc + 1) * 128],
                                  identity=s_iden_f[:, :]).then_inc(tM.h, 1))
                W("scalar", tM, tM.n)
                sM.bump(NDC)
                for dc in range(NDC):
                    last = dc == NDC - 1
                    E("scalar", lambda e, dc=dc, tt=tt, d=dst_fm, last=last:
                      (e.copy(out=d[:, dc * TP3 + 4 + tt * 128: dc * TP3 + 4 + (tt + 1) * 128],
                              in_=psF[:, 5 * 512 + dc * 128: 5 * 512 + (dc + 1) * 128]).then_inc(sM.h, NDC) if last else
                       e.copy(out=d[:, dc * TP3 + 4 + tt * 128: dc * TP3 + 4 + (tt + 1) * 128],
                              in_=psF[:, 5 * 512 + dc * 128: 5 * 512 + (dc + 1) * 128])))
                W("tensor", sM, sM.n)
                ln_tr_t[0] = tM.n

        W("tensor", sFr, CASTS)
        emit_ln(x_tm, xn)
        emit_ln(xp_tm, xnp_s)
        XN_S = sM.n
        W("scalar", dSm, DSM_ALL)
        sM.bump()
        for dc in range(NDC):
            last = dc == NDC - 1
            E("scalar", lambda e, dc=dc, last=last:
              (e.activation(out=xn[:, dc * TP3: dc * TP3 + 4],
                            in_=xnp_s[:, dc * TP3 + T: dc * TP3 + T + 4],
                            func=AF.Copy, scale=s_hg[:, 0:1]).then_inc(sM.h, 1) if last else
               e.activation(out=xn[:, dc * TP3: dc * TP3 + 4],
                            in_=xnp_s[:, dc * TP3 + T: dc * TP3 + T + 4],
                            func=AF.Copy, scale=s_hg[:, 0:1])))
        W("vector", sM, sM.n)
        for dc in range(NDC):
            E("vector", lambda e, dc=dc: e.memzero(xnp_s[:, dc * TP3: dc * TP3 + 4]))
        vM.bump()
        E("vector", lambda e: e.engine_nop().then_inc(vM.h, 1))
        XN_V = vM.n



        xc_spill_v = [0]
        conv_mm_v = [0]
        dltr_t = [0]
        dlsp_v = [0]

        def emit_convproj(pfx):
            src_xn = xnp_s if pfx else xn
            cw = s_pconvw if pfx else s_convw
            cb = s_pcb if pfx else s_cb
            w33b = b_pw33 if pfx else b_w33
            pjt = b_ppjt if pfx else b_pjt
            for tc in range(NTC):
                W("scalar", dSpX, xc_spill_v[0])
                W("tensor", sM, sM.n)
                for dc in range(NDC):
                    W("vector", tM, conv_mm_v[0])
                    vM.bump()

                    def mkdiag(e, dc=dc):
                        for k in range(4):
                            e.tensor_scalar(out=b_cdiag[:, k * 128:(k + 1) * 128],
                                            in0=b_iden[:, :],
                                            scalar1=cw[:, k * NDC + dc: k * NDC + dc + 1],
                                            scalar2=None, op0=OP.mult)
                        e.engine_nop().then_inc(vM.h, 1)
                    E("vector", mkdiag)
                    dgv = vM.n
                    W("tensor", vM, dgv)
                    for k in range(4):
                        tM.bump()
                        E("tensor", lambda e, dc=dc, k=k, tc=tc, s=src_xn:
                          e.matmul(out=FB(5),
                                   lhsT=b_cdiag[:, k * 128:(k + 1) * 128],
                                   rhs=s[:, dc * TP3 + tc * TC + k + 1: dc * TP3 + tc * TC + k + 1 + TC],
                                   start=(k == 0), stop=(k == 3)).then_inc(tM.h, 1))
                    conv_mm_v[0] = tM.n
                    W("scalar", tM, tM.n)
                    sM.bump()
                    E("scalar", lambda e, dc=dc, cb=cb:
                      e.activation(out=b_xc[:, dc * TC:(dc + 1) * TC], in_=FB(5),
                                   func=AF.Silu, bias=cb[:, dc:dc + 1]).then_inc(sM.h, 1))
                    W("tensor", sM, sM.n)
                XCD = sM.n
                W("sync", sM, XCD)
                dSpX.bump(16)
                xc_spill_v[0] = dSpX.n
                E("sync", lambda e, pfx=pfx, tc=tc:
                  e.dma_start(out=xc_dram[1 if pfx else 0, :, :, tc * TC:(tc + 1) * TC]
                              .rearrange("c p t -> p c t"),
                              in_=b_xc[:, :].rearrange("p (c t) -> p c t", c=NDC)).then_inc(dSpX.h, 16))
                for st in range(TC // 128):
                    tt = tc * (TC // 128) + st
                    for dc in range(NDC):
                        tM.bump()
                        E("tensor", lambda e, dc=dc, st=st, w=w33b:
                          e.matmul(out=FB(6, 33),
                                   lhsT=b_xc[:, dc * TC + st * 128: dc * TC + (st + 1) * 128],
                                   rhs=w[:, dc * 33:(dc + 1) * 33],
                                   start=(dc == 0), stop=(dc == NDC - 1)).then_inc(tM.h, 1))
                    PJ = tM.n
                    W("scalar", tM, PJ)
                    sM.bump(2)
                    E("scalar", lambda e, st=st:
                      e.copy(out=s_pjf[:, (st % 2) * 33:(st % 2) * 33 + 33], in_=FB(6, 33)))
                    E("scalar", lambda e, st=st:
                      e.copy(out=s_zc[:, st % 2: st % 2 + 1], in_=FB(6, 33)[:, 32:33]).then_inc(sM.h, 2))
                    ZV = sM.n
                    W("vector", sM, ZV)
                    W("vector", tM, dltr_t[0])
                    vM.bump()
                    E("vector", lambda e, st=st, pfx=pfx:
                      e.scalar_tensor_tensor(out=s_dltm[:, :],
                                             in0=(s_pdtwr if pfx else s_dtwr)[:, :],
                                             scalar=s_zc[:, st % 2: st % 2 + 1],
                                             in1=(s_pdtbr if pfx else s_dtbr)[:, :],
                                             op0=OP.mult, op1=OP.add).then_inc(vM.h, 1))
                    ZETA = vM.n
                    W("scalar", vM, ZETA)
                    sM.bump(3)
                    E("scalar", lambda e: e.activation(out=s_xsq[:, :], in_=s_dltm[:, :], func=AF.Abs))
                    E("scalar", lambda e: e.activation(out=s_xsq[:, :], in_=s_xsq[:, :], func=AF.Exp,
                                                       scale=s_cst[:, C_NEG1:C_NEG1 + 1]))
                    E("scalar", lambda e: e.activation(out=s_xsq[:, :], in_=s_xsq[:, :], func=AF.Ln,
                                                       bias=s_cst[:, C_ONE:C_ONE + 1]).then_inc(sM.h, 3))
                    SPC = sM.n
                    W("vector", sM, SPC)
                    vM.bump()
                    E("vector", lambda e:
                      e.scalar_tensor_tensor(out=s_dltm[:, :], in0=s_dltm[:, :], scalar=0.0,
                                             in1=s_xsq[:, :], op0=OP.max, op1=OP.add).then_inc(vM.h, 1))
                    DLT = vM.n
                    W("tensor", vM, DLT)
                    W("scalar", dSpD, dlsp_v[0])
                    for dc in range(NDC):
                        tM.bump()
                        E("tensor", lambda e, dc=dc:
                          e.transpose(out=psF[:, 4 * 512 + 128:4 * 512 + 256],
                                      in_=s_dltm[:, dc * 128:(dc + 1) * 128],
                                      identity=s_iden_f[:, :]).then_inc(tM.h, 1))
                        W("scalar", tM, tM.n)
                        sM.bump()
                        E("scalar", lambda e, dc=dc:
                          e.copy(out=s_dlfm[:, dc * 128:(dc + 1) * 128],
                                 in_=psF[:, 4 * 512 + 128:4 * 512 + 256]).then_inc(sM.h, 1))
                        W("tensor", sM, sM.n)
                    dltr_t[0] = tM.n
                    DFM = sM.n
                    W("sync", sM, DFM)
                    dSpD.bump(16)
                    dlsp_v[0] = dSpD.n
                    E("sync", lambda e, pfx=pfx, tt=tt:
                      e.dma_start(out=dl_dram[1 if pfx else 0, tt, :, :],
                                  in_=s_dlfm[:, :]).then_inc(dSpD.h, 16))
                    if False and DBG2[0] and not pfx and tt == 0:
                        DBG2[0] = False
                        for dof, srcap in [(0, s_dlfm[:, :]), (1024, s_dltm[:, :]),
                                           (2048, s_xsq[:, :]),
                                           (3072, arf1[:, 0:1024])]:
                            dOut.bump(16)
                            E("sync", lambda e, dof=dof, s=srcap:
                              e.dma_start(out=dbg_m2[:, dof:dof + s.shape[1]],
                                          in_=s).then_inc(dOut.h, 16))
                    tM.bump()
                    E("tensor", lambda e, st=st:
                      e.transpose(out=FB(4, 128)[0:33, :], in_=s_pjf[:, (st % 2) * 33:(st % 2) * 33 + 33],
                                  identity=s_iden_f[:, :]).then_inc(tM.h, 1))
                    W("scalar", tM, tM.n)
                    sM.bump()
                    E("scalar", lambda e, tt=tt, p=pjt:
                      e.copy(out=p[:, tt * 128:(tt + 1) * 128], in_=FB(4, 128)[0:33, :]).then_inc(sM.h, 1))
                    W("tensor", sM, sM.n)
            return sM.n

        W("tensor", dSm, DSM_ALL)
        W("tensor", vM, XN_V)
        if dbg:
            W("sync", vM, XN_V)
            for dof, srcap in [(0, xnp_s[:, 0:1024]), (1024, xnp_s[:, 4 * TP3:4 * TP3 + 1024])]:
                dOut.bump(16)
                E("sync", lambda e, dof=dof, s=srcap:
                  e.dma_start(out=dbg_m3[:, dof:dof + 1024], in_=s).then_inc(dOut.h, 16))

        emit_convproj(False)
        CP_MAIN = sM.n
        emit_convproj(True)
        CP_PFX = sM.n
        ALL_SPX = dSpX.n
        ALL_SPD = dSpD.n

        rt_sig = {}
        scv_all = s_h8[:, 1152:1408]
        s4c_all = s_h8[:, 1408:1664]
        gsc_all = s_h8[:, 1664:1728]
        grkA = s_h8[:, 1728:1792]
        grkB = s_h8[:, 1792:1856]
        p6 = s_h8[:, 1856:2240]
        msk_all = s_h8[:, 1856:2112]
        rnkA = s_h8[:, 2240:2496]
        rnkB = s_h8[:, 2496:2752]
        mrA = s_h8[:, 2752:3008]
        mrB = s_h8[:, 3008:3264]
        wsl_all = s_h8[:, 3264:3520]
        wsm_all = s_h8[:, 3520:3536]
        cmb_all = s_h8[:, 3536:3792]
        ebr_s = s_h8[:, 3792:4048]

        for tt in range(NTT):
            bank = tt % 2
            if tt == 0:
                W("tensor", sFr, CASTS)
                W("tensor", sM, sM.n)
            W("tensor", sM, rt_sig.get(tt - 2, 0))
            for dc in range(NDC):
                tM.bump()
                E("tensor", lambda e, tt=tt, dc=dc, bank=bank:
                  e.matmul(out=FB(4 + bank, 16),
                           lhsT=xn[:, dc * TP3 + 4 + tt * 128: dc * TP3 + 4 + (tt + 1) * 128],
                           rhs=b_gwp[:, dc * 16:(dc + 1) * 16],
                           start=(dc == 0), stop=(dc == NDC - 1)).then_inc(tM.h, 1))
            W("scalar", tM, tM.n)
            sM.bump()
            E("scalar", lambda e, tt=tt, bank=bank:
              e.activation(out=scv_all[:, tt * 16:(tt + 1) * 16], in_=FB(4 + bank, 16),
                           func=AF.Sigmoid).then_inc(sM.h, 1))
            rt_sig[tt] = sM.n
        SIG_ALL = sM.n

        W("vector", sM, SIG_ALL)
        W("vector", dSm, DSM_ALL)
        pairs = [(0, 1), (0, 2), (0, 3), (1, 2), (1, 3), (2, 3)]
        g4 = s4c_all.rearrange("p (tg i) -> p tg i", i=4)
        E("vector", lambda e: e.tensor_add(s4c_all, scv_all, ebr_s))
        SP()
        for k, (i, j) in enumerate(pairs):
            E("vector", lambda e, k=k, i=i, j=j:
              e.tensor_add(p6[:, k * 64:(k + 1) * 64], g4[:, :, i], g4[:, :, j]))
        E("vector", lambda e: e.tensor_max(gsc_all, p6[:, 0:64], p6[:, 64:128]))
        E("vector", lambda e: e.tensor_max(grkB, p6[:, 128:192], p6[:, 192:256]))
        E("vector", lambda e: e.tensor_max(grkA, p6[:, 256:320], p6[:, 320:384]))
        E("vector", lambda e: e.tensor_max(gsc_all, gsc_all, grkB))
        SP()
        E("vector", lambda e: e.tensor_max(gsc_all, gsc_all, grkA))
        SP()
        E("vector", lambda e: e.memset(grkA, 0.0))
        E("vector", lambda e: e.memset(grkB, 0.0))
        gv = gsc_all.rearrange("p (t g) -> p t g", g=4)
        gA = grkA.rearrange("p (t g) -> p t g", g=4)
        gB = grkB.rearrange("p (t g) -> p t g", g=4)
        tA4 = mrA[:, 0:64].rearrange("p (t g) -> p t g", g=4)
        tB4 = mrB[:, 0:64].rearrange("p (t g) -> p t g", g=4)
        for s in range(1, 4):
            E("vector", lambda e, s=s: e.tensor_tensor(out=tA4[:, :, 0:4 - s], in0=gv[:, :, 0:4 - s],
                                                       in1=gv[:, :, s:4], op=OP.is_ge))
            E("vector", lambda e, s=s: e.tensor_tensor(out=tB4[:, :, 0:4 - s], in0=gv[:, :, s:4],
                                                       in1=gv[:, :, 0:4 - s], op=OP.is_gt))
            E("vector", lambda e, s=s: e.tensor_tensor(out=gA[:, :, s:4], in0=gA[:, :, s:4],
                                                       in1=tA4[:, :, 0:4 - s], op=OP.add))
            E("vector", lambda e, s=s: e.tensor_tensor(out=gB[:, :, 0:4 - s], in0=gB[:, :, 0:4 - s],
                                                       in1=tB4[:, :, 0:4 - s], op=OP.add))
        SP()
        E("vector", lambda e: e.tensor_add(grkA, grkA, grkB))
        SP(2)
        E("vector", lambda e: e.tensor_single_scalar(out=grkA, in_=grkA, scalar=2.0, op=OP.is_lt))
        SP(2)
        for i in range(4):
            E("vector", lambda e, i=i: e.tensor_tensor(out=g4[:, :, i], in0=g4[:, :, i],
                                                       in1=grkA, op=OP.mult))
        E("vector", lambda e: e.memset(rnkA, 0.0))
        E("vector", lambda e: e.memset(rnkB, 0.0))
        sv = s4c_all.rearrange("p (t j) -> p t j", j=16)
        vA = rnkA.rearrange("p (t j) -> p t j", j=16)
        vB = rnkB.rearrange("p (t j) -> p t j", j=16)
        tAe = mrA.rearrange("p (t j) -> p t j", j=16)
        tBe = mrB.rearrange("p (t j) -> p t j", j=16)
        for s in range(1, 16):
            w_ = 16 - s
            E("vector", lambda e, s=s, w_=w_: e.tensor_tensor(out=tAe[:, :, 0:w_], in0=sv[:, :, 0:w_],
                                                              in1=sv[:, :, s:16], op=OP.is_ge))
            E("vector", lambda e, s=s, w_=w_: e.tensor_tensor(out=tBe[:, :, 0:w_], in0=sv[:, :, s:16],
                                                              in1=sv[:, :, 0:w_], op=OP.is_gt))
            E("vector", lambda e, s=s, w_=w_: e.tensor_tensor(out=vA[:, :, s:16], in0=vA[:, :, s:16],
                                                              in1=tAe[:, :, 0:w_], op=OP.add))
            E("vector", lambda e, s=s, w_=w_: e.tensor_tensor(out=vB[:, :, 0:w_], in0=vB[:, :, 0:w_],
                                                              in1=tBe[:, :, 0:w_], op=OP.add))
        SP()
        E("vector", lambda e: e.tensor_add(rnkA, rnkA, rnkB))
        SP(2)
        E("vector", lambda e: e.tensor_single_scalar(out=msk_all, in_=rnkA, scalar=4.0, op=OP.is_lt))
        SP(2)
        E("vector", lambda e: e.tensor_mul(wsl_all, scv_all, msk_all))
        SP(2)
        wv = wsl_all.rearrange("p (t j) -> p t j", j=16)
        t8 = mrA[:, 0:128].rearrange("p (t j) -> p t j", j=8)
        t4 = mrB[:, 0:64].rearrange("p (t j) -> p t j", j=4)
        t2 = mrA[:, 128:160].rearrange("p (t j) -> p t j", j=2)
        E("vector", lambda e: e.tensor_tensor(out=t8[:, :, :], in0=wv[:, :, 0:8],
                                              in1=wv[:, :, 8:16], op=OP.add))
        SP()
        E("vector", lambda e: e.tensor_tensor(out=t4[:, :, :], in0=t8[:, :, 0:4],
                                              in1=t8[:, :, 4:8], op=OP.add))
        SP()
        E("vector", lambda e: e.tensor_tensor(out=t2[:, :, :], in0=t4[:, :, 0:2],
                                              in1=t4[:, :, 2:4], op=OP.add))
        SP()
        E("vector", lambda e: e.tensor_tensor(out=wsm_all, in0=t2[:, :, 0],
                                              in1=t2[:, :, 1], op=OP.add))
        SP()
        E("vector", lambda e: e.tensor_scalar_add(wsm_all, wsm_all, 1e-20))
        SP(2)
        E("vector", lambda e: e.reciprocal(wsm_all, wsm_all))
        SP(2)
        cv = cmb_all.rearrange("p (t j) -> p t j", j=16)
        vM.bump()
        for i in range(16):
            last = i == 15
            E("vector", lambda e, i=i, last=last:
              (e.tensor_tensor(out=cv[:, :, i], in0=wv[:, :, i], in1=wsm_all,
                               op=OP.mult).then_inc(vM.h, 1) if last else
               e.tensor_tensor(out=cv[:, :, i], in0=wv[:, :, i], in1=wsm_all,
                               op=OP.mult)))
        CMB_ALL = vM.n
        if dbg:
            W("sync", vM, CMB_ALL)
            dOut.bump(16)
            E("sync", lambda e: e.dma_start(out=dbg_m2[:, 3584:3712],
                                            in_=s_h8[:, 3536:3664]).then_inc(dOut.h, 16))
            W("vector", dOut, dOut.n)

        W("tensor", vM, CMB_ALL)
        rt_ev = [0]
        for tt in range(NTT):
            W("tensor", sM, rt_ev[0])
            tM.bump()
            E("tensor", lambda e, tt=tt:
              e.transpose(out=psF[:, 4 * 512 + 256:4 * 512 + 384][0:16, :],
                          in_=cmb_all[:, tt * 16:(tt + 1) * 16],
                          identity=s_iden_f[:, :]).then_inc(tM.h, 1))
            W("scalar", tM, tM.n)
            sM.bump()
            E("scalar", lambda e, tt=tt:
              e.copy(out=cT[:, tt * 128:(tt + 1) * 128],
                     in_=psF[:, 4 * 512 + 256:4 * 512 + 384][0:16, :]).then_inc(sM.h, 1))
            rt_ev[0] = sM.n
        W("tensor", sM, sM.n)
        ROUTER_S = sM.n

        abk = [0]
        ubk = [0]
        guf_g = [0, 0]
        guf_u = [0, 0]
        crep_free = [0]
        attn_k = [0]

        def emit_bc(tc, pfx):
            W("scalar", gW, gW.n)
            W("scalar", vM, vM.n)
            pjt = b_ppjt if pfx else b_pjt
            targets = [("B", n) for n in range(N)]
            if not pfx:
                targets += [("C", n) for n in range(N)]
            for gi, (kind, n) in enumerate(targets):
                bank = 5 + (gi % 2)
                row = n if kind == "B" else 16 + n
                tM.bump()
                E("tensor", lambda e, row=row, bank=bank, tc=tc, p=pjt:
                  e.matmul(out=FB(bank), lhsT=b_sel33[:, row * 128:(row + 1) * 128],
                           rhs=p[:, tc * TC:(tc + 1) * TC],
                           start=True, stop=True).then_inc(tM.h, 1))
                W("scalar", tM, tM.n)
                dst = b_B if kind == "B" else b_C
                sM.bump()
                E("scalar", lambda e, d=dst, n=n, bank=bank:
                  e.copy(out=d[:, n * TC:(n + 1) * TC], in_=FB(bank)).then_inc(sM.h, 1))
                W("tensor", sM, sM.n)
            return sM.n

        u_done = {}
        ab_done = {}
        scan_col = {}
        pend_sync = []

        def flush_pend():
            for fn in pend_sync:
                fn()
            pend_sync.clear()

        def emit_mixer_dc(tc, dc, pfx, bc_s):
            unit = abk[0] // N
            ring = unit % 2
            app = s_papp if pfx else s_app
            if unit == 0:
                W("sync", dSpX, ALL_SPX)
                W("sync", dSpD, ALL_SPD)
            if unit >= 2:
                W("sync", sAb, ab_done[unit - 2])
                W("sync", vG, u_done[unit - 2])
            dSlD[ring].bump(16)
            E("sync", lambda e, pfx=pfx, tc=tc, dc=dc, ring=ring:
              e.dma_start(out=s_dl[:, ring * TC:(ring + 1) * TC]
                          .rearrange("p (a t) -> p a t", a=4),
                          in_=dl_dram[1 if pfx else 0, tc * 4:(tc + 1) * 4, :, dc * 128:(dc + 1) * 128]
                          .rearrange("a p t -> p a t")).then_inc(dSlD[ring].h, 16))
            dSlX[ring].bump(16)
            E("sync", lambda e, pfx=pfx, tc=tc, dc=dc, ring=ring:
              e.dma_start(out=b_xcl[:, ring * TC:(ring + 1) * TC],
                          in_=xc_dram[1 if pfx else 0, dc, :, tc * TC:(tc + 1) * TC]).then_inc(dSlX[ring].h, 16))
            DLX = (dSlD[ring].n, dSlX[ring].n)
            flush_pend()
            W("vector", dSlD[ring], DLX[0])
            W("vector", dSlX[ring], DLX[1])
            if unit >= 2:
                W("vector", gW, (unit - 1) * N)
            vG.bump()
            E("vector", lambda e, ring=ring:
              e.tensor_mul(s_u[:, ring * TC:(ring + 1) * TC],
                           s_dl[:, ring * TC:(ring + 1) * TC],
                           b_xcl[:, ring * TC:(ring + 1) * TC]).then_inc(vG.h, 1))
            UV = vG.n
            u_done[unit] = UV
            u0 = abk[0]
            W("scalar", dSlD[ring], DLX[0])
            if unit == 0:
                W("scalar", vM, APPN)

            def sc_iters(n0, n1, copy_from=CLAG):
                for n in range(n0, n1):
                    k = u0 + n
                    W("scalar", vScan, k - (RING - 1))
                    if n >= copy_from:
                        kc = k - CLAG
                        E("scalar", lambda e, kc=kc, col=scan_col[kc]:
                          e.copy(out=s_carry[:, col:col + 1],
                                 in_=s_h8[:, (kc % 8) * TCX + TC:(kc % 8) * TCX + TC + 1]))
                    E("scalar", lambda e, k=k, ring=ring, dc=dc, n=n, a=app:
                      e.activation(out=s_abar[:, (k % RING) * TCX + 1:(k % RING) * TCX + 1 + TC],
                                   in_=s_dl[:, ring * TC:(ring + 1) * TC], func=AF.Exp,
                                   scale=a[:, dc * N + n: dc * N + n + 1]))
                    sAb.bump()
                    E("scalar", lambda e, k=k, dc=dc, n=n:
                      e.copy(out=s_wr[:, (k % RING) * TCX:(k % RING) * TCX + 1],
                             in_=s_carry[:, dc * N + n: dc * N + n + 1]).then_inc(sAb.h, 1))
                    scan_col[k] = dc * N + n

            def vec_half(half):
                for j in range(8):
                    n = half * 8 + j
                    k = u0 + n
                    W("vector", sAb, k + 1)
                    W("vector", gW, k + 1)
                    vScan.bump()
                    E("vector", lambda e, k=k, j=j:
                      e.tensor_tensor_scan(out=s_h8[:, j * TCX:j * TCX + 1 + TC],
                                           data0=s_abar[:, (k % RING) * TCX:(k % RING) * TCX + 1 + TC],
                                           data1=s_wr[:, (k % RING) * TCX:(k % RING) * TCX + 1 + TC],
                                           initial=0.0,
                                           op0=OP.mult, op1=OP.add).then_inc(vScan.h, 1))
                if pfx:
                    return
                if half == 0:
                    W("vector", sCp, 2 * unit + 1)
                else:
                    W("vector", sCp, 2 * unit + 2)
                h3 = s_h8[:, :].rearrange("p (n t) -> p n t", n=8)[:, :, 1:1 + TC]
                c3 = b_C[:, half * 8 * TC:(half + 1) * 8 * TC].rearrange("p (n t) -> p n t", n=8)
                E("vector", lambda e, h3=h3, c3=c3:
                  e.tensor_tensor(out=h3, in0=h3, in1=c3, op=OP.mult))
                h4a = h3[:, 0:4, :]
                h4b = h3[:, 4:8, :]
                E("vector", lambda e, h4a=h4a, h4b=h4b:
                  e.tensor_tensor(out=h4a, in0=h4a, in1=h4b, op=OP.add))
                h2a = h3[:, 0:2, :]
                h2b = h3[:, 2:4, :]
                E("vector", lambda e, h2a=h2a, h2b=h2b:
                  e.tensor_tensor(out=h2a, in0=h2a, in1=h2b, op=OP.add))
                h1a = s_h8[:, 1:1 + TC]
                h1b = s_h8[:, TCX + 1:TCX + 1 + TC]
                if half == 0:
                    W("vector", dAT, attn_k[0] * 16)
                    E("vector", lambda e, h1a=h1a, h1b=h1b:
                      e.tensor_add(s_y[:, :], h1a, h1b))
                else:
                    E("vector", lambda e, h1a=h1a, h1b=h1b:
                      e.tensor_add(s_y[:, :], s_y[:, :], h1a))
                    E("vector", lambda e, h1b=h1b:
                      e.tensor_add(s_y[:, :], s_y[:, :], h1b))

            sc_iters(0, 8)
            for kc in range(u0 + 5, u0 + 8):
                W("scalar", vScan, kc + 1)
                if kc == u0 + 7:
                    sCp.bump()
                    E("scalar", lambda e, kc=kc, col=scan_col[kc]:
                      e.copy(out=s_carry[:, col:col + 1],
                             in_=s_h8[:, (kc % 8) * TCX + TC:(kc % 8) * TCX + TC + 1])
                      .then_inc(sCp.h, 1))
                else:
                    E("scalar", lambda e, kc=kc, col=scan_col[kc]:
                      e.copy(out=s_carry[:, col:col + 1],
                             in_=s_h8[:, (kc % 8) * TCX + TC:(kc % 8) * TCX + TC + 1]))
            W("gpsimd", vG, UV)
            W("gpsimd", sM, bc_s)
            for n in range(N):
                k = u0 + n
                W("gpsimd", vScan, k - (RING - 1))
                gW.bump()
                E("gpsimd", lambda e, k=k, n=n, ring=ring:
                  e.tensor_tensor(out=s_wr[:, (k % RING) * TCX + 1:(k % RING) * TCX + 1 + TC],
                                  in0=s_u[:, ring * TC:(ring + 1) * TC],
                                  in1=b_B[:, n * TC:(n + 1) * TC], op=OP.mult).then_inc(gW.h, 1))
            vec_half(0)
            yield
            sc_iters(8, 16, copy_from=11)
            for kc in range(u0 + N - CLAG, u0 + N):
                W("scalar", vScan, kc + 1)
                if kc == u0 + N - 1:
                    sCp.bump()
                    E("scalar", lambda e, kc=kc, col=scan_col[kc]:
                      e.copy(out=s_carry[:, col:col + 1],
                             in_=s_h8[:, (kc % 8) * TCX + TC:(kc % 8) * TCX + TC + 1])
                      .then_inc(sCp.h, 1))
                else:
                    E("scalar", lambda e, kc=kc, col=scan_col[kc]:
                      e.copy(out=s_carry[:, col:col + 1],
                             in_=s_h8[:, (kc % 8) * TCX + TC:(kc % 8) * TCX + TC + 1]))
            ab_done[unit] = sAb.n
            vec_half(1)
            abk[0] += N
            if dbg and DBG_K9[0] and abk[0] >= 16:
                DBG_K9[0] = False
                W("sync", vScan, 9)
                for dof, srcap in [(0, s_wr[:, 2 * TCX:2 * TCX + 16]),
                                   (16, s_abar[:, 2 * TCX:2 * TCX + 16]),
                                   (32, s_h8[:, 0:16]),
                                   (48, s_carry[:, 0:16]),
                                   (64, s_u[:, 0:16]),
                                   (80, s_dl[:, 0:16]),
                                   (96, s_wr[:, 2 * TCX + 496:2 * TCX + 516])]:
                    dOut.bump(16)
                    E("sync", lambda e, dof=dof, s=srcap:
                      e.dma_start(out=dbg_m2[:, 3712 + dof:3712 + dof + s.shape[1]],
                                  in_=s).then_inc(dOut.h, 16))
                W("vector", dOut, dOut.n)
            if DBG_U1[0]:
                DBG_U1[0] = False
                W("sync", vScan, N)
                for dof, srcap in [(0, s_carry[:, :]), (128, s_h8[:, 0:516]),
                                   (644, s_wr[:, 0:1548]), (2192, s_abar[:, 0:1548])]:
                    dOut.bump(16)
                    E("sync", lambda e, dof=dof, s=srcap:
                      e.dma_start(out=dbg_m2[:, dof:dof + s.shape[1]], in_=s).then_inc(dOut.h, 16))
            ak = attn_k[0]
            attn_k[0] += 1
            vM.bump()
            E("vector", lambda e, dc=dc, tc=tc:
              e.scalar_tensor_tensor(out=s_y[:, :],
                                     in0=xn[:, dc * TP3 + 4 + tc * TC: dc * TP3 + 4 + (tc + 1) * TC],
                                     scalar=s_dp[:, dc:dc + 1], in1=s_y[:, :],
                                     op0=OP.mult, op1=OP.add).then_inc(vM.h, 1))
            attn_v = vM.n
            dAT.bump(16)

            def _attn_spill(dc=dc, tc=tc, attn_v=attn_v):
                W("sync", vM, attn_v)
                E("sync", lambda e, dc=dc, tc=tc:
                  e.dma_start(out=attn_dram[dc, :, tc * TC:(tc + 1) * TC],
                              in_=s_y[:, :]).then_inc(dAT.h, 16))
            pend_sync.append(_attn_spill)

        wload_k = [0]

        def emit_wload_item(ei, item):
            slot = ei % 2
            if ei == 0 and item == 0:
                W("sync", tM, TM_PROLOGUE[0])
            dsts = {0: (b_gp0, b_gp1), 1: (b_up0, b_up1)}
            k = wload_k[0]
            wload_k[0] += 1
            ssl = k % 2
            W("sync", sFr, max(CASTS, CASTS + k - 1))
            dW[ssl].bump(16)
            if item < 8:
                which, q = item // 4, item % 4
                src = gp_w if which == 0 else up_w
                E("sync", lambda e, s=src, ei=ei, q=q, ssl=ssl:
                  e.dma_start(out=s_stage[:, ssl * 1024:(ssl + 1) * 1024],
                              in_=s[ei, :, q * 1024:(q + 1) * 1024]).then_inc(dW[ssl].h, 16))
                W("scalar", dW[ssl], dW[ssl].n)
                sFr.bump()
                E("scalar", lambda e, d=dsts[which][slot], q=q, ssl=ssl:
                  e.copy(out=d[:, q * 1024:(q + 1) * 1024],
                         in_=s_stage[:, ssl * 1024:(ssl + 1) * 1024]).then_inc(sFr.h, 1))
            else:
                q = item - 8
                E("sync", lambda e, ei=ei, q=q, ssl=ssl:
                  e.dma_start(out=s_stage[:, ssl * 1024:(ssl + 1) * 1024],
                              in_=dn_w[ei, :, q * 1024:(q + 1) * 1024]).then_inc(dW[ssl].h, 16))
                W("scalar", dW[ssl], dW[ssl].n)
                W("scalar", dDN, (ei * 4 + q) * 16)
                sFr.bump()
                E("scalar", lambda e, q=q, ssl=ssl:
                  e.copy(out=b_dnc[:, :],
                         in_=s_stage[:, ssl * 1024:(ssl + 1) * 1024]).then_inc(sFr.h, 1))
                W("sync", sFr, sFr.n)
                dDN.bump(16)
                E("sync", lambda e, ei=ei, q=q:
                  e.dma_start(out=dn_dram[ei, :, q * 1024:(q + 1) * 1024],
                              in_=b_dnc[:, :]).then_inc(dDN.h, 16))
            return sFr.n

        def emit_wload_chunk(ei, phase):
            wc = sFr.n
            for item in range(phase * 3, phase * 3 + 3):
                wc = emit_wload_item(ei, item)
            return wc

        part2_v = [0]
        upe = [0, 0]

        def emit_expert_part1(ei, sc, wcast):
            slot = ei % 2
            gp_t = b_gp0 if slot == 0 else b_gp1
            up_t = b_up0 if slot == 0 else b_up1
            W("tensor", sFr, wcast)
            W("tensor", vG, crep_free[0])
            W("tensor", sM, ROUTER_S)
            tM.bump()
            E("tensor", lambda e, ei=ei, sc=sc:
              e.matmul(out=FB(4), lhsT=b_sel16[:, ei * 128:(ei + 1) * 128],
                       rhs=cT[:, sc * TC:(sc + 1) * TC], start=True, stop=True).then_inc(tM.h, 1))
            CR = tM.n
            for it in range(4):
                gb = it % 2
                ub = 2 + it % 2
                W("tensor", sM, guf_g[gb])
                W("tensor", sM, upe[ub - 2])
                for dc in range(NDC):
                    tM.bump()
                    E("tensor", lambda e, sc=sc, it=it, dc=dc, gb=gb, w=gp_t:
                      e.matmul(out=FB(gb),
                               lhsT=w[:, dc * II + it * 128: dc * II + (it + 1) * 128],
                               rhs=xn[:, dc * TP3 + 4 + sc * TC: dc * TP3 + 4 + (sc + 1) * TC],
                               start=(dc == 0), stop=(dc == NDC - 1)).then_inc(tM.h, 1))
                GD = tM.n
                for dc in range(NDC):
                    tM.bump()
                    E("tensor", lambda e, sc=sc, it=it, dc=dc, ub=ub, w=up_t:
                      e.matmul(out=FB(ub),
                               lhsT=w[:, dc * II + it * 128: dc * II + (it + 1) * 128],
                               rhs=xn[:, dc * TP3 + 4 + sc * TC: dc * TP3 + 4 + (sc + 1) * TC],
                               start=(dc == 0), stop=(dc == NDC - 1)).then_inc(tM.h, 1))
                UD = tM.n
                W("scalar", tM, GD)
                if it == 0:
                    if ei == 0 and sc == 0:
                        W("scalar", dSpX, ALL_SPX)
                    W("scalar", vM, part2_v[0])
                sM.bump()
                E("scalar", lambda e, gb=gb, it=it:
                  e.activation(out=b_xc[:, it * TC:(it + 1) * TC], in_=FB(gb),
                               func=AF.Silu).then_inc(sM.h, 1))
                guf_g[gb] = sM.n
                W("scalar", tM, UD)
                sM.bump()
                E("scalar", lambda e, ub=ub, it=it:
                  e.copy(out=b_xc[:, (4 + it) * TC:(5 + it) * TC],
                         in_=FB(ub)).then_inc(sM.h, 1))
                upe[ub - 2] = sM.n
            return CR, sM.n

        def emit_expert_part2(ei, sc, st):
            CR, SEV = st
            for it in range(4):
                W("vector", sM, SEV)
                W("vector", tM, CR)
                if ei * 4 + sc >= 1 and it == 0:
                    W("vector", dA, (ei * 4 + sc) * 16)
                E("vector", lambda e, it=it:
                  e.tensor_mul(s_m1[:, :],
                               b_xc[:, it * TC:(it + 1) * TC],
                               b_xc[:, (4 + it) * TC:(5 + it) * TC]))
                vM.bump()
                E("vector", lambda e, it=it:
                  e.tensor_tensor(out=b_act[:, it * TC:(it + 1) * TC],
                                  in0=s_m1[:, :],
                                  in1=FB(4), op=OP.mult).then_inc(vM.h, 1))
            part2_v[0] = vM.n
            vG.bump()
            E("vector", lambda e: e.engine_nop().then_inc(vG.h, 1))
            crep_free[0] = vG.n
            act_v = vM.n
            dA.bump(16)

            def _act_spill(ei=ei, sc=sc, act_v=act_v):
                W("sync", vM, act_v)
                E("sync", lambda e, ei=ei, sc=sc:
                  e.dma_start(out=act_dram[ei, sc, :, :],
                              in_=b_act[:, :]).then_inc(dA.h, 16))
            pend_sync.append(_act_spill)

        TM_PROLOGUE = [tM.n]
        DBG_U1 = [True] if dbg else [False]
        DBG_K9 = [True]

        mix_units = [("p", tc) for tc in range(NTC)] + [("m", tc) for tc in range(NTC)]

        def mixer_gen():
            for kind, tc in mix_units:
                if dbg and kind == "m" and tc == 0:
                    W("sync", vScan, abk[0])
                    dOut.bump(16)
                    E("sync", lambda e: e.dma_start(out=dbg_m2[:, 3840:3968],
                                                    in_=s_carry[:, :]).then_inc(dOut.h, 16))
                    W("vector", dOut, dOut.n)
                bs = emit_bc(tc, kind == "p")
                for dc in range(NDC):
                    g = emit_mixer_dc(tc, dc, kind == "p", bs)
                    next(g)
                    yield
                    for _ in g:
                        pass
                    yield

        def expert_gen():
            wc = 0
            for ph in range(4):
                wc = emit_wload_chunk(0, ph)
            wc_next = 0
            for es in range(NE):
                for sc in range(4):
                    if es + 1 < NE:
                        wc_next = emit_wload_chunk(es + 1, sc)
                    st = emit_expert_part1(es, sc, wc)
                    yield
                    emit_expert_part2(es, sc, st)
                    yield
                wc = wc_next

        mg, eg = mixer_gen(), expert_gen()
        done_m = done_e = False
        while not (done_m and done_e):
            if not done_m:
                try:
                    next(mg)
                except StopIteration:
                    done_m = True
            if not done_e:
                try:
                    next(eg)
                except StopIteration:
                    done_e = True
        flush_pend()

        ALL_ACT = dA.n
        ALL_DN = dDN.n
        ALL_ATT = dAT.n
        PHASEA_T = tM.n
        W("sync", dA, ALL_ACT)
        W("sync", dDN, ALL_DN)
        W("tensor", sM, sM.n)
        W("tensor", vM, vM.n)
        phaseb_free = [0]
        lastw = {}
        asm_done = {}
        asm_fev = {}
        asm_out = {}
        for sc in range(4):
            for ei in range(NE):
                asl = ei % 2
                if ei >= 2:
                    W("sync", tM, lastw[(sc, ei - 2)])
                elif sc > 0:
                    W("sync", tM, lastw[(sc - 1, NE - 2 + ei)])
                else:
                    W("sync", tM, PHASEA_T)
                dAI[asl].bump(16)
                E("sync", lambda e, ei=ei, sc=sc, asl=asl:
                  e.dma_start(out=b_actin[:, asl * 4 * TC:(asl + 1) * 4 * TC],
                              in_=act_dram[ei, sc, :, :]).then_inc(dAI[asl].h, 16))
                AIN = dAI[asl].n
                dDNI[asl].bump(16)
                E("sync", lambda e, ei=ei, asl=asl:
                  e.dma_start(out=b_dnin[:, asl * 4096:(asl + 1) * 4096],
                              in_=dn_dram[ei, :, :]).then_inc(dDNI[asl].h, 16))
                DIN = dDNI[asl].n
                W("tensor", dAI[asl], AIN)
                W("tensor", dDNI[asl], DIN)
                if ei == 0:
                    W("tensor", sM, phaseb_free[0])
                for ic in range(4):
                    for ht in range(NDC):
                        oap = FB(ht)
                        tM.bump()
                        E("tensor", lambda e, ei=ei, ic=ic, ht=ht, asl=asl, o=oap:
                          e.matmul(out=o,
                                   lhsT=b_dnin[:, asl * 4096 + ic * 1024 + ht * 128:
                                               asl * 4096 + ic * 1024 + (ht + 1) * 128],
                                   rhs=b_actin[:, (asl * 4 + ic) * TC:(asl * 4 + ic + 1) * TC],
                                   start=(ei == 0 and ic == 0),
                                   stop=(ei == NE - 1 and ic == 3)).then_inc(tM.h, 1))
                lastw[(sc, ei)] = tM.n
            DOWN = tM.n
            W("sync", dAT, ALL_ATT)
            for ht in range(NDC):
                k = sc * NDC + ht
                sl = k % 2
                if k >= 2:
                    W("sync", vM, asm_done.get(k - 2, 0))
                dATI[sl].bump(32)
                E("sync", lambda e, ht=ht, sc=sc, sl=sl:
                  e.dma_start(out=s_ati[:, sl * TC:(sl + 1) * TC],
                              in_=attn_dram[ht, :, sc * TC:(sc + 1) * TC]).then_inc(dATI[sl].h, 16))
                E("sync", lambda e, ht=ht, sc=sc, sl=sl:
                  e.dma_start(out=s_xi[:, sl * TC:(sl + 1) * TC],
                              in_=x_fm[ht * 128:(ht + 1) * 128, sc * TC:(sc + 1) * TC]).then_inc(dATI[sl].h, 16))
                ATI = dATI[sl].n
                W("scalar", tM, DOWN)
                W("scalar", vM, asm_fev.get(k - 2, 0))
                oap = FB(ht)
                sM.bump()
                E("scalar", lambda e, ht=ht, sl=sl, o=oap:
                  e.copy(out=s_fev[:, sl * TC:(sl + 1) * TC], in_=o).then_inc(sM.h, 1))
                FEV = sM.n
                W("vector", sM, FEV)
                W("vector", dATI[sl], ATI)
                W("vector", dOut, asm_out.get(k - 2, 0))
                E("vector", lambda e, sl=sl:
                  e.tensor_add(s_o1[:, sl * TC:(sl + 1) * TC], s_fev[:, sl * TC:(sl + 1) * TC],
                               s_ati[:, sl * TC:(sl + 1) * TC]))
                vM.bump()
                E("vector", lambda e, sl=sl:
                  e.tensor_add(s_o1[:, sl * TC:(sl + 1) * TC], s_o1[:, sl * TC:(sl + 1) * TC],
                               s_xi[:, sl * TC:(sl + 1) * TC]).then_inc(vM.h, 1))
                OA = vM.n
                asm_done[k] = OA
                asm_fev[k] = OA
                W("sync", vM, OA)
                dOut.bump(16)
                asm_out[k] = dOut.n
                E("sync", lambda e, ht=ht, sc=sc, sl=sl:
                  e.dma_start(out=y_out[ht * 128:(ht + 1) * 128, sc * TC:(sc + 1) * TC],
                              in_=s_o1[:, sl * TC:(sl + 1) * TC]).then_inc(dOut.h, 16))
            phaseb_free[0] = sM.n

        W("sync", dOut, dOut.n)
        if dbg:
            dbv = [0]

            def dump(dst, srcap):
                dbv[0] += 16
                E("sync", lambda e, d=dst, s=srcap: e.dma_start(out=d, in_=s).then_inc(dOut.h, 16))
                dOut.n += 16
            dump(dbg_xn, xn[:, :])
            dump(dbg_ct, cT[:, :])
            dump(dbg_bb, b_B[:, :])
            dump(dbg_cc, b_C[:, :])
            dump(dbg_misc[:, 0:128], s_carry[:, :])
            dump(dbg_misc[:, 128:144], s_stat[:, :])
            dump(dbg_misc[:, 144:272], s_app[:, :])
            dump(dbg_misc[:, 272:400], s_papp[:, :])
            dump(dbg_misc[:, 400:912], s_y[:, :])
            dump(dbg_misc[:, 912:1424], s_y[:, :])
            dump(dbg_misc[:, 1424:1936], s_abar[:, 0:512])
            dump(dbg_misc[:, 1936:2448], s_wr[:, 0:512])
            dump(dbg_misc[:, 2448:2960], s_u[:, 0:512])
            dump(dbg_misc[:, 2960:3472], s_dl[:, 0:512])
            W("sync", dOut, dOut.n)

        with nc.Block() as block:
            @block.sync
            def _(e):
                for f in ops["sync"]:
                    f(e)

            @block.tensor
            def _(e):
                for f in ops["tensor"]:
                    f(e)

            @block.vector
            def _(e):
                for f in ops["vector"]:
                    f(e)

            @block.scalar
            def _(e):
                for f in ops["scalar"]:
                    f(e)

            @block.gpsimd
            def _(e):
                for f in ops["gpsimd"]:
                    f(e)

    return nc


_NC_CACHE = {}


def _packs(inputs):
    f = np.float32
    conv_w = np.asarray(inputs["L0_attn_conv_w"], f)
    conv_b = np.asarray(inputs["L0_attn_conv_b"], f)
    xproj = np.asarray(inputs["L0_attn_xproj_w"], f)
    dt_w = np.asarray(inputs["L0_attn_dt_w"], f)
    dt_b = np.asarray(inputs["L0_attn_dt_b"], f)
    a_log = np.asarray(inputs["L0_attn_a_log"], f)
    Dp = np.asarray(inputs["L0_attn_Dp"], f)
    gw = np.asarray(inputs["L0_ffn_gate_weight"], f)
    e_bias = np.asarray(inputs["L0_ffn_e_bias"], f)
    gp = np.asarray(inputs["L0_ffn_gate_proj"], f)
    up = np.asarray(inputs["L0_ffn_up_proj"], f)
    dn = np.asarray(inputs["L0_ffn_down_proj"], f)

    def fm_pack(v):
        return np.ascontiguousarray(v.reshape(NDC, 128).T)

    d = {}
    d["w33"] = np.ascontiguousarray(
        xproj.T.reshape(NDC, 128, 33).transpose(1, 0, 2).reshape(128, NDC * 33))
    d["gwp"] = np.ascontiguousarray(
        gw.T.reshape(NDC, 128, 16).transpose(1, 0, 2).reshape(128, NDC * 16))
    d["convw"] = np.ascontiguousarray(
        conv_w[:, 0, :].reshape(NDC, 128, 4).transpose(1, 2, 0).reshape(128, 4 * NDC))
    d["cb_pp"] = fm_pack(conv_b)
    d["dtw_rep"] = np.ascontiguousarray(np.tile(dt_w[:, 0][None, :], (128, 1)))
    d["dtb_rep"] = np.ascontiguousarray(np.tile(dt_b[None, :], (128, 1)))
    d["alog"] = np.ascontiguousarray(
        a_log.reshape(NDC, 128, N).transpose(1, 0, 2).reshape(128, NDC * N))
    d["dp_pp"] = fm_pack(Dp)
    d["ebias"] = np.ascontiguousarray(np.tile(e_bias[None, :], (128, 1)))
    d["ebr"] = np.ascontiguousarray(np.tile(e_bias[None, :], (128, NTT)))
    d["sel33"] = np.ascontiguousarray(np.repeat(np.eye(33, dtype=f), 128, axis=1))
    d["sel16"] = np.ascontiguousarray(np.repeat(np.eye(16, dtype=f), 128, axis=1))
    d["iden"] = np.eye(128, dtype=f)
    cst = np.zeros((128, 8), f)
    cst[:, C_NEG1] = -1.0
    cst[:, C_ONE] = 1.0
    cst[:, C_EPS] = 1e-5
    cst[:, C_HALF] = 0.5
    d["csts"] = cst
    d["gp_w"] = np.ascontiguousarray(
        gp.transpose(0, 2, 1).reshape(NE, NDC, 128, II).transpose(0, 2, 1, 3).reshape(NE, 128, NDC * II))
    d["up_w"] = np.ascontiguousarray(
        up.transpose(0, 2, 1).reshape(NE, NDC, 128, II).transpose(0, 2, 1, 3).reshape(NE, 128, NDC * II))
    d["dn_w"] = np.ascontiguousarray(
        dn.transpose(0, 2, 1).reshape(NE, 4, 128, D).transpose(0, 2, 1, 3).reshape(NE, 128, 4 * D))
    d["p_w33"] = d["w33"]
    d["p_convw"] = d["convw"]
    d["p_cb_pp"] = d["cb_pp"]
    d["p_dtw_rep"] = d["dtw_rep"]
    d["p_dtb_rep"] = d["dtb_rep"]
    d["p_alog"] = d["alog"]
    return d


def kernel(**inputs):
    inputs = dict(inputs)
    x = np.asarray(inputs["L0_attn_x"], np.float32)
    shared = _packs(inputs)
    zero_like = {k: np.zeros_like(v) for k, v in shared.items()
                 if k.startswith("p_")}

    in_maps = []
    for c in range(8):
        b, hf = c // 2, c % 2
        m = dict(shared)
        xp = np.ascontiguousarray(x[b, 0:T])
        if hf == 0:
            m.update(zero_like)
        m["hgate"] = np.full((128, 1), float(hf), np.float32)
        m["xp_tm"] = xp
        m["x_tm"] = np.ascontiguousarray(x[b, hf * T:(hf + 1) * T])
        m["x_fm"] = np.ascontiguousarray(x[b, hf * T:(hf + 1) * T].T)
        in_maps.append(m)

    dbg = bool(inputs.get("_dbg"))
    key = "nc_dbg" if dbg else "nc"
    if key not in _NC_CACHE:
        _NC_CACHE[key] = build_nc(dbg)
    nc = _NC_CACHE[key]
    res = run_bass_kernel_spmd(nc, in_maps, core_ids=list(range(8)))

    out = np.empty((B, L, D), np.float32)
    for c in range(8):
        b, hf = c // 2, c % 2
        out[b, hf * T:(hf + 1) * T, :] = np.asarray(res.results[c]["y_fm"]).T
    if dbg:
        return out, res.results
    return out


# version 16
# speedup vs baseline: 1.0480x; 1.0480x over previous
from contextlib import ExitStack
import numpy as np

import concourse.bass as bass
import concourse.mybir as mybir
from concourse.bass_utils import run_bass_kernel_spmd

F32 = mybir.dt.float32
BF16 = mybir.dt.bfloat16
AF = mybir.ActivationFunctionType
OP = mybir.AluOpType
AX = mybir.AxisListType

B, L, D, N, NE, II = 4, 4096, 1024, 16, 16, 512
T = 2048
NDC = 8
TC = 512
NTC = T // TC
NTT = T // 128
TP3 = T + 4
RING = 3
CLAG = RING
BIG = -1e30
MARK = 1e30

C_NEG1, C_ONE, C_EPS, C_HALF = 0, 1, 2, 3


class S:
    def __init__(self, nc, ctx, name):
        self.h = ctx.enter_context(nc.semaphore(name))
        self.n = 0

    def bump(self, k=1):
        self.n += k
        return self.n


def build_nc(dbg=False):
    DBG2 = [dbg]
    nc = bass.Bass("TRN2", target_bir_lowering=False, detect_race_conditions=False)
    ops = {k: [] for k in ("sync", "tensor", "vector", "scalar", "gpsimd")}

    def E(eng, fn):
        ops[eng].append(fn)

    def W(eng, sem, val):
        if val > 0:
            ops[eng].append(lambda e, s=sem.h, v=val: e.wait_ge(s, v))

    def din(name, shape, dt=F32):
        return nc.dram_tensor(name, shape, dt, kind="ExternalInput").ap()

    x_tm = din("x_tm", [T, D])
    xp_tm = din("xp_tm", [T, D])
    x_fm = din("x_fm", [D, T])
    w33 = din("w33", [128, NDC * 33])
    p_w33 = din("p_w33", [128, NDC * 33])
    gwp = din("gwp", [128, NDC * 16])
    convw = din("convw", [128, 4 * NDC])
    p_convw = din("p_convw", [128, 4 * NDC])
    cb_pp = din("cb_pp", [128, NDC])
    p_cb_pp = din("p_cb_pp", [128, NDC])
    dtw_rep = din("dtw_rep", [128, D])
    dtb_rep = din("dtb_rep", [128, D])
    p_dtw_rep = din("p_dtw_rep", [128, D])
    p_dtb_rep = din("p_dtb_rep", [128, D])
    alog = din("alog", [128, NDC * N])
    p_alog = din("p_alog", [128, NDC * N])
    dp_pp = din("dp_pp", [128, NDC])
    ebias = din("ebias", [128, NE])
    ebr = din("ebr", [128, NE * NTT])
    hgate = din("hgate", [128, 1])
    sel33 = din("sel33", [33, 33 * 128])
    sel16 = din("sel16", [16, 16 * 128])
    iden = din("iden", [128, 128])
    csts = din("csts", [128, 8])
    gp_w = din("gp_w", [NE, 128, NDC * II])
    up_w = din("up_w", [NE, 128, NDC * II])
    dn_w = din("dn_w", [NE, 128, 4 * D])

    KI = "ExternalOutput" if dbg else "Internal"
    act_dram = nc.dram_tensor("act_dram", [NE, NTC, 128, 4 * TC], BF16, kind=KI).ap()
    dn_dram = nc.dram_tensor("dn_dram", [NE, 128, 4 * D], BF16, kind=KI).ap()
    attn_dram = nc.dram_tensor("attn_dram", [NDC, 128, T], F32, kind=KI).ap()
    xc_dram = nc.dram_tensor("xc_dram", [2, NDC, 128, T], BF16, kind=KI).ap()
    dl_dram = nc.dram_tensor("dl_dram", [2, NTT, 128, NDC * 128], F32, kind=KI).ap()
    y_out = nc.dram_tensor("y_fm", [D, T], F32, kind="ExternalOutput").ap()
    if dbg:
        dbg_xn = nc.dram_tensor("dbg_xn", [128, NDC * TP3], BF16, kind="ExternalOutput").ap()
        dbg_misc = nc.dram_tensor("dbg_misc", [128, 4096], F32, kind="ExternalOutput").ap()
        dbg_ct = nc.dram_tensor("dbg_ct", [16, T], BF16, kind="ExternalOutput").ap()
        dbg_bb = nc.dram_tensor("dbg_bb", [128, N * TC], BF16, kind="ExternalOutput").ap()
        dbg_cc = nc.dram_tensor("dbg_cc", [128, N * TC], BF16, kind="ExternalOutput").ap()
        dbg_m2 = nc.dram_tensor("dbg_m2", [128, 4096], F32, kind="ExternalOutput").ap()
        dbg_m3 = nc.dram_tensor("dbg_m3", [128, 2048], BF16, kind="ExternalOutput").ap()

    with ExitStack() as ctx:
        def sb(name, shape, dt=F32):
            return ctx.enter_context(nc.sbuf_tensor(name, shape, dt))

        b_w33 = sb("b_w33", [128, NDC * 33], BF16)
        b_pw33 = sb("b_pw33", [128, NDC * 33], BF16)
        b_gwp = sb("b_gwp", [128, NDC * 16], BF16)
        s_convw = sb("s_convw", [128, 4 * NDC])
        s_pconvw = sb("s_pconvw", [128, 4 * NDC])
        b_cdiag = sb("b_cdiag", [128, 4 * 128], BF16)
        s_cb = sb("s_cb", [128, NDC])
        s_pcb = sb("s_pcb", [128, NDC])
        arf1 = sb("arf1", [128, 4096])
        s_dtwr = arf1[:, 0:1024]
        s_dtbr = arf1[:, 1024:2048]
        s_pdtwr = arf1[:, 2048:3072]
        s_pdtbr = arf1[:, 3072:4096]
        s_app = sb("s_app", [128, NDC * N])
        s_papp = sb("s_papp", [128, NDC * N])
        s_dp = sb("s_dp", [128, NDC])
        s_eb = sb("s_eb", [128, NE])
        b_sel33 = sb("b_sel33", [33, 33 * 128], BF16)
        b_sel16 = sb("b_sel16", [16, 16 * 128], BF16)
        b_iden = sb("b_iden", [128, 128], BF16)
        s_cst = sb("s_cst", [128, 8])
        s_hg = sb("s_hg", [128, 1])

        s_stat = sb("s_stat", [128, 16])
        s_zc = sb("s_zc", [128, 2])

        xn = sb("xn", [128, NDC * TP3], BF16)
        b_pjt = sb("b_pjt", [33, T], BF16)
        b_ppjt = sb("b_ppjt", [33, T], BF16)
        cT = sb("cT", [16, T], BF16)
        s_rt = sb("s_rt", [128, 112])
        s_cmb = sb("s_cmb", [128, NE])
        b_xc = sb("b_xc", [128, NDC * TC], BF16)

        b_B = sb("b_B", [128, N * TC], BF16)
        b_C = sb("b_C", [128, N * TC], BF16)
        TCX = TC + 4
        s_abar = sb("s_abar", [128, RING * TCX])
        s_wr = sb("s_wr", [128, RING * TCX])
        s_u = sb("s_u", [128, 2 * TC])
        s_dl = sb("s_dl", [128, 2 * TC])
        s_xnt = s_dl[:, 0:1024]
        b_xcl = sb("b_xcl", [128, 2 * TC], BF16)
        s_h8 = sb("s_h8", [128, 8 * TCX])
        s_xsq = s_wr[:, 0:1024]
        s_dltm = s_abar[:, 0:1024]
        s_dlfm = s_h8[:, 0:1024]
        s_pjf = s_abar[:, 1024:1090]
        s_carry = sb("s_carry", [128, 128])

        arf2 = sb("arf2", [128, 1536])
        s_xin = arf2[:, 0:1024]
        s_silu = arf2[:, 0:1024]
        s_m1 = arf2[:, 1024:1536]
        s_stage = sb("s_stage", [128, 2 * 1024])
        barena = sb("barena", [128, NDC * TP3], BF16)
        xnp_s = barena[:, :]
        b_gp0 = barena[:, 0 * 4096:1 * 4096]
        b_gp1 = barena[:, 1 * 4096:2 * 4096]
        b_up0 = barena[:, 2 * 4096:3 * 4096]
        b_up1 = barena[:, 3 * 4096:4 * 4096]
        b_actin = barena[:, 0:2 * 4 * TC]
        b_dnin = barena[:, 2 * 4 * TC:2 * 4 * TC + 2 * 4096]
        b_dnc = sb("b_dnc", [128, 1024], BF16)
        b_act = sb("b_act", [128, 4 * TC], BF16)

        s_y = arf1[:, 1024:1536]
        s_fev = arf1[:, 0:1024]
        s_ati = arf1[:, 1024:2048]
        s_xi = arf1[:, 2048:3072]
        s_o1 = arf1[:, 3072:4096]

        psF = ctx.enter_context(nc.psum_tensor("psF", [128, 8 * 512], F32))

        def FB(i, w=512):
            return psF[:, i * 512: i * 512 + w]

        sprc = s_h8[:, 4048:4112]

        def SP(n=1):
            for _ in range(n):
                E("vector", lambda e: e.tensor_scalar_mul(sprc, sprc, 1.0))

        dSm = S(nc, ctx, "dSm")
        dX = S(nc, ctx, "dX")
        dW = [S(nc, ctx, "dW0"), S(nc, ctx, "dW1")]
        dA = S(nc, ctx, "dA")
        dAI = [S(nc, ctx, "dAI0"), S(nc, ctx, "dAI1")]
        dDN = S(nc, ctx, "dDN")
        dDNI = [S(nc, ctx, "dDNI0"), S(nc, ctx, "dDNI1")]
        dAT = S(nc, ctx, "dAT")
        dATI = [S(nc, ctx, "dATI0"), S(nc, ctx, "dATI1")]
        dSpX = S(nc, ctx, "dSpX")
        dSpD = S(nc, ctx, "dSpD")
        dSlD = [S(nc, ctx, "dSlD0"), S(nc, ctx, "dSlD1")]
        dSlX = [S(nc, ctx, "dSlX0"), S(nc, ctx, "dSlX1")]
        dOut = S(nc, ctx, "dOut")
        tM = S(nc, ctx, "tM")
        vM = S(nc, ctx, "vM")
        sM = S(nc, ctx, "sM")
        vScan = S(nc, ctx, "vScan")
        sAb = S(nc, ctx, "sAb")
        sCp = S(nc, ctx, "sCp")
        gW = S(nc, ctx, "gW")
        vG = S(nc, ctx, "vG")
        sFr = S(nc, ctx, "sFr")

        s_iden_f = s_h8[:, 1024:1152]
        for dst, src in [(s_convw, convw), (s_pconvw, p_convw), (s_cb, cb_pp),
                         (s_pcb, p_cb_pp), (s_dtwr, dtw_rep), (s_dtbr, dtb_rep),
                         (s_pdtwr, p_dtw_rep), (s_pdtbr, p_dtb_rep),
                         (s_dp, dp_pp), (s_eb, ebias), (s_cst, csts),
                         (s_hg, hgate), (s_app, alog), (s_papp, p_alog),
                         (s_iden_f, iden), (s_h8[:, 3792:4048], ebr)]:
            dSm.bump(16)
            E("sync", lambda e, d=dst, s=src: e.dma_start(out=d[:, :], in_=s).then_inc(dSm.h, 16))
        SMALLS = DSM_ALL = dSm.n

        stage_k = [0]

        def stage_load(src_ap, pp, ww, cast_out_ap, inc_sems=()):
            k = stage_k[0]
            stage_k[0] += 1
            sl = k % 2
            W("sync", sFr, k - 1)
            dW[sl].bump(16)
            E("sync", lambda e, s=src_ap, sl=sl, p=pp, w=ww:
              e.dma_start(out=s_stage[:p, sl * 1024: sl * 1024 + w], in_=s).then_inc(dW[sl].h, 16))
            W("scalar", dW[sl], dW[sl].n)
            sFr.bump()
            E("scalar", lambda e, d=cast_out_ap, sl=sl, p=pp, w=ww, inc=tuple(inc_sems):
              e.copy(out=d, in_=s_stage[:p, sl * 1024: sl * 1024 + w]).then_inc(sFr.h, 1))

        for dst, src, pp, ftot in [(b_w33, w33, 128, NDC * 33), (b_pw33, p_w33, 128, NDC * 33),
                                   (b_gwp, gwp, 128, NDC * 16),
                                   (b_sel33, sel33, 33, 33 * 128), (b_sel16, sel16, 16, 16 * 128),
                                   (b_iden, iden, 128, 128)]:
            off = 0
            while off < ftot:
                w = min(1024, ftot - off)
                stage_load(src[:, off:off + w], pp, w, dst[:pp, off:off + w])
                off += w
        CASTS = sFr.n

        W("scalar", dSm, DSM_ALL)
        E("scalar", lambda e: e.activation(out=s_app[:, :], in_=s_app[:, :], func=AF.Exp))
        sM.bump(1)
        E("scalar", lambda e: e.activation(out=s_papp[:, :], in_=s_papp[:, :], func=AF.Exp).then_inc(sM.h, 1))
        APPX = sM.n
        W("vector", sM, APPX)
        E("vector", lambda e: e.tensor_scalar_mul(s_app[:, :], s_app[:, :], -1.0))
        E("vector", lambda e: e.tensor_scalar_mul(s_papp[:, :], s_papp[:, :], -1.0))
        E("vector", lambda e: e.memzero(s_stat[:, :]))
        E("vector", lambda e: e.memzero(s_rt[:, :]))
        for sl in range(RING):
            E("vector", lambda e, sl=sl: e.memzero(s_abar[:, sl * TCX:sl * TCX + 2]))
            E("vector", lambda e, sl=sl: e.memzero(s_wr[:, sl * TCX + 1 + TC:(sl + 1) * TCX]))
            E("vector", lambda e, sl=sl: e.memzero(s_abar[:, sl * TCX + 1 + TC:(sl + 1) * TCX]))
        E("vector", lambda e: e.memzero(sprc[:, :]))
        vM.bump()
        E("vector", lambda e: e.memzero(s_carry[:, :]).then_inc(vM.h, 1))
        APPN = vM.n



        ln_last_v = [0]
        ln_tr_t = [0]

        def emit_ln(src_dram, dst_fm):
            for tt in range(NTT):
                sl = 0
                W("sync", vM, ln_last_v[0])
                W("sync", tM, ln_tr_t[0])
                dX.bump(16)
                E("sync", lambda e, s=src_dram, tt=tt:
                  e.dma_start(out=s_xin[:, 0:1024],
                              in_=s[tt * 128:(tt + 1) * 128, :]).then_inc(dX.h, 16))
                xv = dX.n
                W("vector", dX, xv)
                W("scalar", dX, xv)
                sM.bump()
                E("scalar", lambda e:
                  e.activation(out=s_xsq[:, :], in_=s_xin[:, 0:1024],
                               func=AF.Square).then_inc(sM.h, 1))
                sqv = sM.n
                E("vector", lambda e:
                  e.reduce_sum(s_stat[:, 0:1], s_xin[:, 0:1024], axis=AX.X))
                W("vector", sM, sqv)
                E("vector", lambda e: e.reduce_sum(s_stat[:, 1:2], s_xsq[:, :], axis=AX.X))
                E("vector", lambda e: e.tensor_scalar_mul(s_stat[:, 2:3], s_stat[:, 0:1], 1.0 / D))
                E("vector", lambda e: e.tensor_scalar_mul(s_stat[:, 3:4], s_stat[:, 1:2], 1.0 / D))
                E("vector", lambda e: e.tensor_mul(s_stat[:, 4:5], s_stat[:, 2:3], s_stat[:, 2:3]))
                SP()
                vM.bump()
                E("vector", lambda e: e.tensor_sub(s_stat[:, 5:6], s_stat[:, 3:4],
                                                   s_stat[:, 4:5]).then_inc(vM.h, 1))
                vv = vM.n
                W("scalar", vM, vv)
                sM.bump()
                E("scalar", lambda e: e.activation(out=s_stat[:, 6:7], in_=s_stat[:, 5:6],
                                                   func=AF.Sqrt,
                                                   bias=s_cst[:, C_EPS:C_EPS + 1]).then_inc(sM.h, 1))
                sv = sM.n
                W("vector", sM, sv)
                E("vector", lambda e: e.reciprocal(s_stat[:, 7:8], s_stat[:, 6:7]))
                SP()
                W("vector", tM, ln_tr_t[0])
                vM.bump()
                E("vector", lambda e:
                  e.tensor_scalar(out=s_xnt[:, :], in0=s_xin[:, 0:1024],
                                  scalar1=s_stat[:, 2:3], scalar2=s_stat[:, 7:8],
                                  op0=OP.subtract, op1=OP.mult).then_inc(vM.h, 1))
                vv = vM.n
                ln_last_v[0] = vv
                W("tensor", vM, vv)
                for dc in range(NDC):
                    tM.bump()
                    E("tensor", lambda e, dc=dc:
                      e.transpose(out=psF[:, 5 * 512 + dc * 128: 5 * 512 + (dc + 1) * 128],
                                  in_=s_xnt[:, dc * 128:(dc + 1) * 128],
                                  identity=s_iden_f[:, :]).then_inc(tM.h, 1))
                W("scalar", tM, tM.n)
                sM.bump(NDC)
                for dc in range(NDC):
                    last = dc == NDC - 1
                    E("scalar", lambda e, dc=dc, tt=tt, d=dst_fm, last=last:
                      (e.copy(out=d[:, dc * TP3 + 4 + tt * 128: dc * TP3 + 4 + (tt + 1) * 128],
                              in_=psF[:, 5 * 512 + dc * 128: 5 * 512 + (dc + 1) * 128]).then_inc(sM.h, NDC) if last else
                       e.copy(out=d[:, dc * TP3 + 4 + tt * 128: dc * TP3 + 4 + (tt + 1) * 128],
                              in_=psF[:, 5 * 512 + dc * 128: 5 * 512 + (dc + 1) * 128])))
                W("tensor", sM, sM.n)
                ln_tr_t[0] = tM.n

        W("tensor", sFr, CASTS)
        emit_ln(x_tm, xn)
        emit_ln(xp_tm, xnp_s)
        XN_S = sM.n
        W("scalar", dSm, DSM_ALL)
        sM.bump()
        for dc in range(NDC):
            last = dc == NDC - 1
            E("scalar", lambda e, dc=dc, last=last:
              (e.activation(out=xn[:, dc * TP3: dc * TP3 + 4],
                            in_=xnp_s[:, dc * TP3 + T: dc * TP3 + T + 4],
                            func=AF.Copy, scale=s_hg[:, 0:1]).then_inc(sM.h, 1) if last else
               e.activation(out=xn[:, dc * TP3: dc * TP3 + 4],
                            in_=xnp_s[:, dc * TP3 + T: dc * TP3 + T + 4],
                            func=AF.Copy, scale=s_hg[:, 0:1])))
        W("vector", sM, sM.n)
        for dc in range(NDC):
            E("vector", lambda e, dc=dc: e.memzero(xnp_s[:, dc * TP3: dc * TP3 + 4]))
        vM.bump()
        E("vector", lambda e: e.engine_nop().then_inc(vM.h, 1))
        XN_V = vM.n



        xc_spill_v = [0]
        conv_mm_v = [0]
        dltr_t = [0]
        dlsp_v = [0]

        def emit_convproj(pfx):
            src_xn = xnp_s if pfx else xn
            cw = s_pconvw if pfx else s_convw
            cb = s_pcb if pfx else s_cb
            w33b = b_pw33 if pfx else b_w33
            pjt = b_ppjt if pfx else b_pjt
            for tc in range(NTC):
                W("scalar", dSpX, xc_spill_v[0])
                W("tensor", sM, sM.n)
                for dc in range(NDC):
                    W("vector", tM, conv_mm_v[0])
                    vM.bump()

                    def mkdiag(e, dc=dc):
                        for k in range(4):
                            e.tensor_scalar(out=b_cdiag[:, k * 128:(k + 1) * 128],
                                            in0=b_iden[:, :],
                                            scalar1=cw[:, k * NDC + dc: k * NDC + dc + 1],
                                            scalar2=None, op0=OP.mult)
                        e.engine_nop().then_inc(vM.h, 1)
                    E("vector", mkdiag)
                    dgv = vM.n
                    W("tensor", vM, dgv)
                    for k in range(4):
                        tM.bump()
                        E("tensor", lambda e, dc=dc, k=k, tc=tc, s=src_xn:
                          e.matmul(out=FB(5),
                                   lhsT=b_cdiag[:, k * 128:(k + 1) * 128],
                                   rhs=s[:, dc * TP3 + tc * TC + k + 1: dc * TP3 + tc * TC + k + 1 + TC],
                                   start=(k == 0), stop=(k == 3)).then_inc(tM.h, 1))
                    conv_mm_v[0] = tM.n
                    W("scalar", tM, tM.n)
                    sM.bump()
                    E("scalar", lambda e, dc=dc, cb=cb:
                      e.activation(out=b_xc[:, dc * TC:(dc + 1) * TC], in_=FB(5),
                                   func=AF.Silu, bias=cb[:, dc:dc + 1]).then_inc(sM.h, 1))
                    W("tensor", sM, sM.n)
                XCD = sM.n
                W("sync", sM, XCD)
                dSpX.bump(16)
                xc_spill_v[0] = dSpX.n
                E("sync", lambda e, pfx=pfx, tc=tc:
                  e.dma_start(out=xc_dram[1 if pfx else 0, :, :, tc * TC:(tc + 1) * TC]
                              .rearrange("c p t -> p c t"),
                              in_=b_xc[:, :].rearrange("p (c t) -> p c t", c=NDC)).then_inc(dSpX.h, 16))
                for st in range(TC // 128):
                    tt = tc * (TC // 128) + st
                    for dc in range(NDC):
                        tM.bump()
                        E("tensor", lambda e, dc=dc, st=st, w=w33b:
                          e.matmul(out=FB(6, 33),
                                   lhsT=b_xc[:, dc * TC + st * 128: dc * TC + (st + 1) * 128],
                                   rhs=w[:, dc * 33:(dc + 1) * 33],
                                   start=(dc == 0), stop=(dc == NDC - 1)).then_inc(tM.h, 1))
                    PJ = tM.n
                    W("scalar", tM, PJ)
                    sM.bump(2)
                    E("scalar", lambda e, st=st:
                      e.copy(out=s_pjf[:, (st % 2) * 33:(st % 2) * 33 + 33], in_=FB(6, 33)))
                    E("scalar", lambda e, st=st:
                      e.copy(out=s_zc[:, st % 2: st % 2 + 1], in_=FB(6, 33)[:, 32:33]).then_inc(sM.h, 2))
                    ZV = sM.n
                    W("vector", sM, ZV)
                    W("vector", tM, dltr_t[0])
                    vM.bump()
                    E("vector", lambda e, st=st, pfx=pfx:
                      e.scalar_tensor_tensor(out=s_dltm[:, :],
                                             in0=(s_pdtwr if pfx else s_dtwr)[:, :],
                                             scalar=s_zc[:, st % 2: st % 2 + 1],
                                             in1=(s_pdtbr if pfx else s_dtbr)[:, :],
                                             op0=OP.mult, op1=OP.add).then_inc(vM.h, 1))
                    ZETA = vM.n
                    W("scalar", vM, ZETA)
                    sM.bump(3)
                    E("scalar", lambda e: e.activation(out=s_xsq[:, :], in_=s_dltm[:, :], func=AF.Abs))
                    E("scalar", lambda e: e.activation(out=s_xsq[:, :], in_=s_xsq[:, :], func=AF.Exp,
                                                       scale=s_cst[:, C_NEG1:C_NEG1 + 1]))
                    E("scalar", lambda e: e.activation(out=s_xsq[:, :], in_=s_xsq[:, :], func=AF.Ln,
                                                       bias=s_cst[:, C_ONE:C_ONE + 1]).then_inc(sM.h, 3))
                    SPC = sM.n
                    W("vector", sM, SPC)
                    vM.bump()
                    E("vector", lambda e:
                      e.scalar_tensor_tensor(out=s_dltm[:, :], in0=s_dltm[:, :], scalar=0.0,
                                             in1=s_xsq[:, :], op0=OP.max, op1=OP.add).then_inc(vM.h, 1))
                    DLT = vM.n
                    W("tensor", vM, DLT)
                    W("scalar", dSpD, dlsp_v[0])
                    for dc in range(NDC):
                        tM.bump()
                        E("tensor", lambda e, dc=dc:
                          e.transpose(out=psF[:, 4 * 512 + 128:4 * 512 + 256],
                                      in_=s_dltm[:, dc * 128:(dc + 1) * 128],
                                      identity=s_iden_f[:, :]).then_inc(tM.h, 1))
                        W("scalar", tM, tM.n)
                        sM.bump()
                        E("scalar", lambda e, dc=dc:
                          e.copy(out=s_dlfm[:, dc * 128:(dc + 1) * 128],
                                 in_=psF[:, 4 * 512 + 128:4 * 512 + 256]).then_inc(sM.h, 1))
                        W("tensor", sM, sM.n)
                    dltr_t[0] = tM.n
                    DFM = sM.n
                    W("sync", sM, DFM)
                    dSpD.bump(16)
                    dlsp_v[0] = dSpD.n
                    E("sync", lambda e, pfx=pfx, tt=tt:
                      e.dma_start(out=dl_dram[1 if pfx else 0, tt, :, :],
                                  in_=s_dlfm[:, :]).then_inc(dSpD.h, 16))
                    if False and DBG2[0] and not pfx and tt == 0:
                        DBG2[0] = False
                        for dof, srcap in [(0, s_dlfm[:, :]), (1024, s_dltm[:, :]),
                                           (2048, s_xsq[:, :]),
                                           (3072, arf1[:, 0:1024])]:
                            dOut.bump(16)
                            E("sync", lambda e, dof=dof, s=srcap:
                              e.dma_start(out=dbg_m2[:, dof:dof + s.shape[1]],
                                          in_=s).then_inc(dOut.h, 16))
                    tM.bump()
                    E("tensor", lambda e, st=st:
                      e.transpose(out=FB(4, 128)[0:33, :], in_=s_pjf[:, (st % 2) * 33:(st % 2) * 33 + 33],
                                  identity=s_iden_f[:, :]).then_inc(tM.h, 1))
                    W("scalar", tM, tM.n)
                    sM.bump()
                    E("scalar", lambda e, tt=tt, p=pjt:
                      e.copy(out=p[:, tt * 128:(tt + 1) * 128], in_=FB(4, 128)[0:33, :]).then_inc(sM.h, 1))
                    W("tensor", sM, sM.n)
            return sM.n

        W("tensor", dSm, DSM_ALL)
        W("tensor", vM, XN_V)
        if dbg:
            W("sync", vM, XN_V)
            for dof, srcap in [(0, xnp_s[:, 0:1024]), (1024, xnp_s[:, 4 * TP3:4 * TP3 + 1024])]:
                dOut.bump(16)
                E("sync", lambda e, dof=dof, s=srcap:
                  e.dma_start(out=dbg_m3[:, dof:dof + 1024], in_=s).then_inc(dOut.h, 16))

        emit_convproj(False)
        CP_MAIN = sM.n
        emit_convproj(True)
        CP_PFX = sM.n
        ALL_SPX = dSpX.n
        ALL_SPD = dSpD.n

        rt_sig = {}
        scv_all = s_h8[:, 1152:1408]
        s4c_all = s_h8[:, 1408:1664]
        gsc_all = s_h8[:, 1664:1728]
        grkA = s_h8[:, 1728:1792]
        grkB = s_h8[:, 1792:1856]
        p6 = s_h8[:, 1856:2240]
        msk_all = s_h8[:, 1856:2112]
        rnkA = s_h8[:, 2240:2496]
        rnkB = s_h8[:, 2496:2752]
        mrA = s_h8[:, 2752:3008]
        mrB = s_h8[:, 3008:3264]
        wsl_all = s_h8[:, 3264:3520]
        wsm_all = s_h8[:, 3520:3536]
        cmb_all = s_h8[:, 3536:3792]
        ebr_s = s_h8[:, 3792:4048]

        for tt in range(NTT):
            bank = tt % 2
            if tt == 0:
                W("tensor", sFr, CASTS)
                W("tensor", sM, sM.n)
            W("tensor", sM, rt_sig.get(tt - 2, 0))
            for dc in range(NDC):
                tM.bump()
                E("tensor", lambda e, tt=tt, dc=dc, bank=bank:
                  e.matmul(out=FB(4 + bank, 16),
                           lhsT=xn[:, dc * TP3 + 4 + tt * 128: dc * TP3 + 4 + (tt + 1) * 128],
                           rhs=b_gwp[:, dc * 16:(dc + 1) * 16],
                           start=(dc == 0), stop=(dc == NDC - 1)).then_inc(tM.h, 1))
            W("scalar", tM, tM.n)
            sM.bump()
            E("scalar", lambda e, tt=tt, bank=bank:
              e.activation(out=scv_all[:, tt * 16:(tt + 1) * 16], in_=FB(4 + bank, 16),
                           func=AF.Sigmoid).then_inc(sM.h, 1))
            rt_sig[tt] = sM.n
        SIG_ALL = sM.n

        W("vector", sM, SIG_ALL)
        W("vector", dSm, DSM_ALL)
        pairs = [(0, 1), (0, 2), (0, 3), (1, 2), (1, 3), (2, 3)]
        g4 = s4c_all.rearrange("p (tg i) -> p tg i", i=4)
        E("vector", lambda e: e.tensor_add(s4c_all, scv_all, ebr_s))
        SP()
        for k, (i, j) in enumerate(pairs):
            E("vector", lambda e, k=k, i=i, j=j:
              e.tensor_add(p6[:, k * 64:(k + 1) * 64], g4[:, :, i], g4[:, :, j]))
        E("vector", lambda e: e.tensor_max(gsc_all, p6[:, 0:64], p6[:, 64:128]))
        E("vector", lambda e: e.tensor_max(grkB, p6[:, 128:192], p6[:, 192:256]))
        E("vector", lambda e: e.tensor_max(grkA, p6[:, 256:320], p6[:, 320:384]))
        E("vector", lambda e: e.tensor_max(gsc_all, gsc_all, grkB))
        SP()
        E("vector", lambda e: e.tensor_max(gsc_all, gsc_all, grkA))
        SP()
        E("vector", lambda e: e.memset(grkA, 0.0))
        E("vector", lambda e: e.memset(grkB, 0.0))
        gv = gsc_all.rearrange("p (t g) -> p t g", g=4)
        gA = grkA.rearrange("p (t g) -> p t g", g=4)
        gB = grkB.rearrange("p (t g) -> p t g", g=4)
        tA4 = mrA[:, 0:64].rearrange("p (t g) -> p t g", g=4)
        tB4 = mrB[:, 0:64].rearrange("p (t g) -> p t g", g=4)
        for s in range(1, 4):
            E("vector", lambda e, s=s: e.tensor_tensor(out=tA4[:, :, 0:4 - s], in0=gv[:, :, 0:4 - s],
                                                       in1=gv[:, :, s:4], op=OP.is_ge))
            E("vector", lambda e, s=s: e.tensor_tensor(out=tB4[:, :, 0:4 - s], in0=gv[:, :, s:4],
                                                       in1=gv[:, :, 0:4 - s], op=OP.is_gt))
            E("vector", lambda e, s=s: e.tensor_tensor(out=gA[:, :, s:4], in0=gA[:, :, s:4],
                                                       in1=tA4[:, :, 0:4 - s], op=OP.add))
            E("vector", lambda e, s=s: e.tensor_tensor(out=gB[:, :, 0:4 - s], in0=gB[:, :, 0:4 - s],
                                                       in1=tB4[:, :, 0:4 - s], op=OP.add))
        SP()
        E("vector", lambda e: e.tensor_add(grkA, grkA, grkB))
        SP(2)
        E("vector", lambda e: e.tensor_single_scalar(out=grkA, in_=grkA, scalar=2.0, op=OP.is_lt))
        SP(2)
        for i in range(4):
            E("vector", lambda e, i=i: e.tensor_tensor(out=g4[:, :, i], in0=g4[:, :, i],
                                                       in1=grkA, op=OP.mult))
        E("vector", lambda e: e.memset(rnkA, 0.0))
        E("vector", lambda e: e.memset(rnkB, 0.0))
        sv = s4c_all.rearrange("p (t j) -> p t j", j=16)
        vA = rnkA.rearrange("p (t j) -> p t j", j=16)
        vB = rnkB.rearrange("p (t j) -> p t j", j=16)
        tAe = mrA.rearrange("p (t j) -> p t j", j=16)
        tBe = mrB.rearrange("p (t j) -> p t j", j=16)
        for s in range(1, 16):
            w_ = 16 - s
            E("vector", lambda e, s=s, w_=w_: e.tensor_tensor(out=tAe[:, :, 0:w_], in0=sv[:, :, 0:w_],
                                                              in1=sv[:, :, s:16], op=OP.is_ge))
            E("vector", lambda e, s=s, w_=w_: e.tensor_tensor(out=tBe[:, :, 0:w_], in0=sv[:, :, s:16],
                                                              in1=sv[:, :, 0:w_], op=OP.is_gt))
            E("vector", lambda e, s=s, w_=w_: e.tensor_tensor(out=vA[:, :, s:16], in0=vA[:, :, s:16],
                                                              in1=tAe[:, :, 0:w_], op=OP.add))
            E("vector", lambda e, s=s, w_=w_: e.tensor_tensor(out=vB[:, :, 0:w_], in0=vB[:, :, 0:w_],
                                                              in1=tBe[:, :, 0:w_], op=OP.add))
        SP()
        E("vector", lambda e: e.tensor_add(rnkA, rnkA, rnkB))
        SP(2)
        E("vector", lambda e: e.tensor_single_scalar(out=msk_all, in_=rnkA, scalar=4.0, op=OP.is_lt))
        SP(2)
        E("vector", lambda e: e.tensor_mul(wsl_all, scv_all, msk_all))
        SP(2)
        wv = wsl_all.rearrange("p (t j) -> p t j", j=16)
        t8 = mrA[:, 0:128].rearrange("p (t j) -> p t j", j=8)
        t4 = mrB[:, 0:64].rearrange("p (t j) -> p t j", j=4)
        t2 = mrA[:, 128:160].rearrange("p (t j) -> p t j", j=2)
        E("vector", lambda e: e.tensor_tensor(out=t8[:, :, :], in0=wv[:, :, 0:8],
                                              in1=wv[:, :, 8:16], op=OP.add))
        SP()
        E("vector", lambda e: e.tensor_tensor(out=t4[:, :, :], in0=t8[:, :, 0:4],
                                              in1=t8[:, :, 4:8], op=OP.add))
        SP()
        E("vector", lambda e: e.tensor_tensor(out=t2[:, :, :], in0=t4[:, :, 0:2],
                                              in1=t4[:, :, 2:4], op=OP.add))
        SP()
        E("vector", lambda e: e.tensor_tensor(out=wsm_all, in0=t2[:, :, 0],
                                              in1=t2[:, :, 1], op=OP.add))
        SP()
        E("vector", lambda e: e.tensor_scalar_add(wsm_all, wsm_all, 1e-20))
        SP(2)
        E("vector", lambda e: e.reciprocal(wsm_all, wsm_all))
        SP(2)
        cv = cmb_all.rearrange("p (t j) -> p t j", j=16)
        vM.bump()
        for i in range(16):
            last = i == 15
            E("vector", lambda e, i=i, last=last:
              (e.tensor_tensor(out=cv[:, :, i], in0=wv[:, :, i], in1=wsm_all,
                               op=OP.mult).then_inc(vM.h, 1) if last else
               e.tensor_tensor(out=cv[:, :, i], in0=wv[:, :, i], in1=wsm_all,
                               op=OP.mult)))
        CMB_ALL = vM.n
        if dbg:
            W("sync", vM, CMB_ALL)
            dOut.bump(16)
            E("sync", lambda e: e.dma_start(out=dbg_m2[:, 3584:3712],
                                            in_=s_h8[:, 3536:3664]).then_inc(dOut.h, 16))
            W("vector", dOut, dOut.n)

        W("tensor", vM, CMB_ALL)
        rt_ev = [0]
        for tt in range(NTT):
            W("tensor", sM, rt_ev[0])
            tM.bump()
            E("tensor", lambda e, tt=tt:
              e.transpose(out=psF[:, 4 * 512 + 256:4 * 512 + 384][0:16, :],
                          in_=cmb_all[:, tt * 16:(tt + 1) * 16],
                          identity=s_iden_f[:, :]).then_inc(tM.h, 1))
            W("scalar", tM, tM.n)
            sM.bump()
            E("scalar", lambda e, tt=tt:
              e.copy(out=cT[:, tt * 128:(tt + 1) * 128],
                     in_=psF[:, 4 * 512 + 256:4 * 512 + 384][0:16, :]).then_inc(sM.h, 1))
            rt_ev[0] = sM.n
        W("tensor", sM, sM.n)
        ROUTER_S = sM.n

        abk = [0]
        ubk = [0]
        guf_g = [0, 0]
        guf_u = [0, 0]
        crep_free = [0]
        attn_k = [0]

        def emit_bc(tc, pfx):
            W("scalar", gW, gW.n)
            W("scalar", vM, vM.n)
            pjt = b_ppjt if pfx else b_pjt
            targets = [("B", n) for n in range(N)]
            if not pfx:
                targets += [("C", n) for n in range(N)]
            for gi, (kind, n) in enumerate(targets):
                bank = 5 + (gi % 2)
                row = n if kind == "B" else 16 + n
                tM.bump()
                E("tensor", lambda e, row=row, bank=bank, tc=tc, p=pjt:
                  e.matmul(out=FB(bank), lhsT=b_sel33[:, row * 128:(row + 1) * 128],
                           rhs=p[:, tc * TC:(tc + 1) * TC],
                           start=True, stop=True).then_inc(tM.h, 1))
                W("scalar", tM, tM.n)
                dst = b_B if kind == "B" else b_C
                sM.bump()
                E("scalar", lambda e, d=dst, n=n, bank=bank:
                  e.copy(out=d[:, n * TC:(n + 1) * TC], in_=FB(bank)).then_inc(sM.h, 1))
                W("tensor", sM, sM.n)
            return sM.n

        u_done = {}
        ab_done = {}
        scan_col = {}
        pend_sync = []

        def flush_pend():
            for fn in pend_sync:
                fn()
            pend_sync.clear()

        def emit_mixer_dc(tc, dc, pfx, bc_s):
            unit = abk[0] // N
            ring = unit % 2
            app = s_papp if pfx else s_app
            if unit == 0:
                W("sync", dSpX, ALL_SPX)
                W("sync", dSpD, ALL_SPD)
            if unit >= 2:
                W("sync", sAb, ab_done[unit - 2])
                W("sync", vG, u_done[unit - 2])
            dSlD[ring].bump(16)
            E("sync", lambda e, pfx=pfx, tc=tc, dc=dc, ring=ring:
              e.dma_start(out=s_dl[:, ring * TC:(ring + 1) * TC]
                          .rearrange("p (a t) -> p a t", a=4),
                          in_=dl_dram[1 if pfx else 0, tc * 4:(tc + 1) * 4, :, dc * 128:(dc + 1) * 128]
                          .rearrange("a p t -> p a t")).then_inc(dSlD[ring].h, 16))
            dSlX[ring].bump(16)
            E("sync", lambda e, pfx=pfx, tc=tc, dc=dc, ring=ring:
              e.dma_start(out=b_xcl[:, ring * TC:(ring + 1) * TC],
                          in_=xc_dram[1 if pfx else 0, dc, :, tc * TC:(tc + 1) * TC]).then_inc(dSlX[ring].h, 16))
            DLX = (dSlD[ring].n, dSlX[ring].n)
            flush_pend()
            W("vector", dSlD[ring], DLX[0])
            W("vector", dSlX[ring], DLX[1])
            if unit >= 2:
                W("vector", gW, (unit - 1) * N)
            vG.bump()
            E("vector", lambda e, ring=ring:
              e.tensor_mul(s_u[:, ring * TC:(ring + 1) * TC],
                           s_dl[:, ring * TC:(ring + 1) * TC],
                           b_xcl[:, ring * TC:(ring + 1) * TC]).then_inc(vG.h, 1))
            UV = vG.n
            u_done[unit] = UV
            u0 = abk[0]
            W("scalar", dSlD[ring], DLX[0])
            if unit == 0:
                W("scalar", vM, APPN)

            def sc_iters(n0, n1, copy_from=CLAG):
                for n in range(n0, n1):
                    k = u0 + n
                    W("scalar", vScan, k - (RING - 1))
                    if n >= copy_from:
                        kc = k - CLAG
                        if kc == u0 + 7:
                            sCp.bump()
                            E("scalar", lambda e, kc=kc, col=scan_col[kc]:
                              e.copy(out=s_carry[:, col:col + 1],
                                     in_=s_h8[:, (kc % 8) * TCX + TC:(kc % 8) * TCX + TC + 1])
                              .then_inc(sCp.h, 1))
                        else:
                            E("scalar", lambda e, kc=kc, col=scan_col[kc]:
                              e.copy(out=s_carry[:, col:col + 1],
                                     in_=s_h8[:, (kc % 8) * TCX + TC:(kc % 8) * TCX + TC + 1]))
                    E("scalar", lambda e, k=k, ring=ring, dc=dc, n=n, a=app:
                      e.activation(out=s_abar[:, (k % RING) * TCX + 1:(k % RING) * TCX + 1 + TC],
                                   in_=s_dl[:, ring * TC:(ring + 1) * TC], func=AF.Exp,
                                   scale=a[:, dc * N + n: dc * N + n + 1]))
                    sAb.bump()
                    E("scalar", lambda e, k=k, dc=dc, n=n:
                      e.copy(out=s_wr[:, (k % RING) * TCX:(k % RING) * TCX + 1],
                             in_=s_carry[:, dc * N + n: dc * N + n + 1]).then_inc(sAb.h, 1))
                    scan_col[k] = dc * N + n

            def vec_half(half):
                for j in range(8):
                    n = half * 8 + j
                    k = u0 + n
                    W("vector", sAb, k + 1)
                    W("vector", gW, k + 1)
                    vScan.bump()
                    E("vector", lambda e, k=k, j=j:
                      e.tensor_tensor_scan(out=s_h8[:, j * TCX:j * TCX + 1 + TC],
                                           data0=s_abar[:, (k % RING) * TCX:(k % RING) * TCX + 1 + TC],
                                           data1=s_wr[:, (k % RING) * TCX:(k % RING) * TCX + 1 + TC],
                                           initial=0.0,
                                           op0=OP.mult, op1=OP.add).then_inc(vScan.h, 1))
                if pfx:
                    return
                if half == 0:
                    W("vector", sCp, 2 * unit + 1)
                else:
                    W("vector", sCp, 2 * unit + 2)
                h3 = s_h8[:, :].rearrange("p (n t) -> p n t", n=8)[:, :, 1:1 + TC]
                c3 = b_C[:, half * 8 * TC:(half + 1) * 8 * TC].rearrange("p (n t) -> p n t", n=8)
                E("vector", lambda e, h3=h3, c3=c3:
                  e.tensor_tensor(out=h3, in0=h3, in1=c3, op=OP.mult))
                h4a = h3[:, 0:4, :]
                h4b = h3[:, 4:8, :]
                E("vector", lambda e, h4a=h4a, h4b=h4b:
                  e.tensor_tensor(out=h4a, in0=h4a, in1=h4b, op=OP.add))
                h2a = h3[:, 0:2, :]
                h2b = h3[:, 2:4, :]
                E("vector", lambda e, h2a=h2a, h2b=h2b:
                  e.tensor_tensor(out=h2a, in0=h2a, in1=h2b, op=OP.add))
                h1a = s_h8[:, 1:1 + TC]
                h1b = s_h8[:, TCX + 1:TCX + 1 + TC]
                if half == 0:
                    W("vector", dAT, attn_k[0] * 16)
                    E("vector", lambda e, h1a=h1a, h1b=h1b:
                      e.tensor_add(s_y[:, :], h1a, h1b))
                else:
                    E("vector", lambda e, h1a=h1a, h1b=h1b:
                      e.tensor_add(s_y[:, :], s_y[:, :], h1a))
                    E("vector", lambda e, h1b=h1b:
                      e.tensor_add(s_y[:, :], s_y[:, :], h1b))

            sc_iters(0, 11)
            W("gpsimd", vG, UV)
            W("gpsimd", sM, bc_s)
            for n in range(N):
                k = u0 + n
                W("gpsimd", vScan, k - (RING - 1))
                gW.bump()
                E("gpsimd", lambda e, k=k, n=n, ring=ring:
                  e.tensor_tensor(out=s_wr[:, (k % RING) * TCX + 1:(k % RING) * TCX + 1 + TC],
                                  in0=s_u[:, ring * TC:(ring + 1) * TC],
                                  in1=b_B[:, n * TC:(n + 1) * TC], op=OP.mult).then_inc(gW.h, 1))
            vec_half(0)
            yield
            sc_iters(11, 16)
            for kc in range(u0 + N - CLAG, u0 + N):
                W("scalar", vScan, kc + 1)
                if kc == u0 + N - 1:
                    sCp.bump()
                    E("scalar", lambda e, kc=kc, col=scan_col[kc]:
                      e.copy(out=s_carry[:, col:col + 1],
                             in_=s_h8[:, (kc % 8) * TCX + TC:(kc % 8) * TCX + TC + 1])
                      .then_inc(sCp.h, 1))
                else:
                    E("scalar", lambda e, kc=kc, col=scan_col[kc]:
                      e.copy(out=s_carry[:, col:col + 1],
                             in_=s_h8[:, (kc % 8) * TCX + TC:(kc % 8) * TCX + TC + 1]))
            ab_done[unit] = sAb.n
            vec_half(1)
            abk[0] += N
            if dbg and DBG_K9[0] and abk[0] >= 16:
                DBG_K9[0] = False
                W("sync", vScan, 9)
                for dof, srcap in [(0, s_wr[:, 2 * TCX:2 * TCX + 16]),
                                   (16, s_abar[:, 2 * TCX:2 * TCX + 16]),
                                   (32, s_h8[:, 0:16]),
                                   (48, s_carry[:, 0:16]),
                                   (64, s_u[:, 0:16]),
                                   (80, s_dl[:, 0:16]),
                                   (96, s_wr[:, 2 * TCX + 496:2 * TCX + 516])]:
                    dOut.bump(16)
                    E("sync", lambda e, dof=dof, s=srcap:
                      e.dma_start(out=dbg_m2[:, 3712 + dof:3712 + dof + s.shape[1]],
                                  in_=s).then_inc(dOut.h, 16))
                W("vector", dOut, dOut.n)
            if DBG_U1[0]:
                DBG_U1[0] = False
                W("sync", vScan, N)
                for dof, srcap in [(0, s_carry[:, :]), (128, s_h8[:, 0:516]),
                                   (644, s_wr[:, 0:1548]), (2192, s_abar[:, 0:1548])]:
                    dOut.bump(16)
                    E("sync", lambda e, dof=dof, s=srcap:
                      e.dma_start(out=dbg_m2[:, dof:dof + s.shape[1]], in_=s).then_inc(dOut.h, 16))
            ak = attn_k[0]
            attn_k[0] += 1
            vM.bump()
            E("vector", lambda e, dc=dc, tc=tc:
              e.scalar_tensor_tensor(out=s_y[:, :],
                                     in0=xn[:, dc * TP3 + 4 + tc * TC: dc * TP3 + 4 + (tc + 1) * TC],
                                     scalar=s_dp[:, dc:dc + 1], in1=s_y[:, :],
                                     op0=OP.mult, op1=OP.add).then_inc(vM.h, 1))
            attn_v = vM.n
            dAT.bump(16)

            def _attn_spill(dc=dc, tc=tc, attn_v=attn_v):
                W("sync", vM, attn_v)
                E("sync", lambda e, dc=dc, tc=tc:
                  e.dma_start(out=attn_dram[dc, :, tc * TC:(tc + 1) * TC],
                              in_=s_y[:, :]).then_inc(dAT.h, 16))
            pend_sync.append(_attn_spill)

        wload_k = [0]

        def emit_wload_item(ei, item):
            slot = ei % 2
            if ei == 0 and item == 0:
                W("sync", tM, TM_PROLOGUE[0])
            dsts = {0: (b_gp0, b_gp1), 1: (b_up0, b_up1)}
            k = wload_k[0]
            wload_k[0] += 1
            ssl = k % 2
            W("sync", sFr, max(CASTS, CASTS + k - 1))
            dW[ssl].bump(16)
            if item < 8:
                which, q = item // 4, item % 4
                src = gp_w if which == 0 else up_w
                E("sync", lambda e, s=src, ei=ei, q=q, ssl=ssl:
                  e.dma_start(out=s_stage[:, ssl * 1024:(ssl + 1) * 1024],
                              in_=s[ei, :, q * 1024:(q + 1) * 1024]).then_inc(dW[ssl].h, 16))
                W("scalar", dW[ssl], dW[ssl].n)
                sFr.bump()
                E("scalar", lambda e, d=dsts[which][slot], q=q, ssl=ssl:
                  e.copy(out=d[:, q * 1024:(q + 1) * 1024],
                         in_=s_stage[:, ssl * 1024:(ssl + 1) * 1024]).then_inc(sFr.h, 1))
            else:
                q = item - 8
                E("sync", lambda e, ei=ei, q=q, ssl=ssl:
                  e.dma_start(out=s_stage[:, ssl * 1024:(ssl + 1) * 1024],
                              in_=dn_w[ei, :, q * 1024:(q + 1) * 1024]).then_inc(dW[ssl].h, 16))
                W("scalar", dW[ssl], dW[ssl].n)
                W("scalar", dDN, (ei * 4 + q) * 16)
                sFr.bump()
                E("scalar", lambda e, q=q, ssl=ssl:
                  e.copy(out=b_dnc[:, :],
                         in_=s_stage[:, ssl * 1024:(ssl + 1) * 1024]).then_inc(sFr.h, 1))
                W("sync", sFr, sFr.n)
                dDN.bump(16)
                E("sync", lambda e, ei=ei, q=q:
                  e.dma_start(out=dn_dram[ei, :, q * 1024:(q + 1) * 1024],
                              in_=b_dnc[:, :]).then_inc(dDN.h, 16))
            return sFr.n

        def emit_wload_chunk(ei, phase):
            wc = sFr.n
            for item in range(phase * 3, phase * 3 + 3):
                wc = emit_wload_item(ei, item)
            return wc

        part2_v = [0]
        upe = [0, 0]

        def emit_expert_part1(ei, sc, wcast):
            slot = ei % 2
            gp_t = b_gp0 if slot == 0 else b_gp1
            up_t = b_up0 if slot == 0 else b_up1
            W("tensor", sFr, wcast)
            W("tensor", vG, crep_free[0])
            W("tensor", sM, ROUTER_S)
            tM.bump()
            E("tensor", lambda e, ei=ei, sc=sc:
              e.matmul(out=FB(4), lhsT=b_sel16[:, ei * 128:(ei + 1) * 128],
                       rhs=cT[:, sc * TC:(sc + 1) * TC], start=True, stop=True).then_inc(tM.h, 1))
            CR = tM.n
            for it in range(4):
                gb = it % 2
                ub = 2 + it % 2
                W("tensor", sM, guf_g[gb])
                W("tensor", sM, upe[ub - 2])
                for dc in range(NDC):
                    tM.bump()
                    E("tensor", lambda e, sc=sc, it=it, dc=dc, gb=gb, w=gp_t:
                      e.matmul(out=FB(gb),
                               lhsT=w[:, dc * II + it * 128: dc * II + (it + 1) * 128],
                               rhs=xn[:, dc * TP3 + 4 + sc * TC: dc * TP3 + 4 + (sc + 1) * TC],
                               start=(dc == 0), stop=(dc == NDC - 1)).then_inc(tM.h, 1))
                GD = tM.n
                for dc in range(NDC):
                    tM.bump()
                    E("tensor", lambda e, sc=sc, it=it, dc=dc, ub=ub, w=up_t:
                      e.matmul(out=FB(ub),
                               lhsT=w[:, dc * II + it * 128: dc * II + (it + 1) * 128],
                               rhs=xn[:, dc * TP3 + 4 + sc * TC: dc * TP3 + 4 + (sc + 1) * TC],
                               start=(dc == 0), stop=(dc == NDC - 1)).then_inc(tM.h, 1))
                UD = tM.n
                W("scalar", tM, GD)
                if it == 0:
                    if ei == 0 and sc == 0:
                        W("scalar", dSpX, ALL_SPX)
                    W("scalar", vM, part2_v[0])
                sM.bump()
                E("scalar", lambda e, gb=gb, it=it:
                  e.activation(out=b_xc[:, it * TC:(it + 1) * TC], in_=FB(gb),
                               func=AF.Silu).then_inc(sM.h, 1))
                guf_g[gb] = sM.n
                W("scalar", tM, UD)
                sM.bump()
                E("scalar", lambda e, ub=ub, it=it:
                  e.copy(out=b_xc[:, (4 + it) * TC:(5 + it) * TC],
                         in_=FB(ub)).then_inc(sM.h, 1))
                upe[ub - 2] = sM.n
            return CR, sM.n

        def emit_expert_part2(ei, sc, st):
            CR, SEV = st
            for it in range(4):
                W("vector", sM, SEV)
                W("vector", tM, CR)
                if ei * 4 + sc >= 1 and it == 0:
                    W("vector", dA, (ei * 4 + sc) * 16)
                E("vector", lambda e, it=it:
                  e.tensor_mul(s_m1[:, :],
                               b_xc[:, it * TC:(it + 1) * TC],
                               b_xc[:, (4 + it) * TC:(5 + it) * TC]))
                vM.bump()
                E("vector", lambda e, it=it:
                  e.tensor_tensor(out=b_act[:, it * TC:(it + 1) * TC],
                                  in0=s_m1[:, :],
                                  in1=FB(4), op=OP.mult).then_inc(vM.h, 1))
            part2_v[0] = vM.n
            vG.bump()
            E("vector", lambda e: e.engine_nop().then_inc(vG.h, 1))
            crep_free[0] = vG.n
            act_v = vM.n
            dA.bump(16)

            def _act_spill(ei=ei, sc=sc, act_v=act_v):
                W("sync", vM, act_v)
                E("sync", lambda e, ei=ei, sc=sc:
                  e.dma_start(out=act_dram[ei, sc, :, :],
                              in_=b_act[:, :]).then_inc(dA.h, 16))
            pend_sync.append(_act_spill)

        TM_PROLOGUE = [tM.n]
        DBG_U1 = [True] if dbg else [False]
        DBG_K9 = [True]

        mix_units = [("p", tc) for tc in range(NTC)] + [("m", tc) for tc in range(NTC)]

        def mixer_gen():
            for kind, tc in mix_units:
                if dbg and kind == "m" and tc == 0:
                    W("sync", vScan, abk[0])
                    dOut.bump(16)
                    E("sync", lambda e: e.dma_start(out=dbg_m2[:, 3840:3968],
                                                    in_=s_carry[:, :]).then_inc(dOut.h, 16))
                    W("vector", dOut, dOut.n)
                bs = emit_bc(tc, kind == "p")
                for dc in range(NDC):
                    g = emit_mixer_dc(tc, dc, kind == "p", bs)
                    next(g)
                    yield
                    for _ in g:
                        pass
                    yield

        def expert_gen():
            wc = 0
            for ph in range(4):
                wc = emit_wload_chunk(0, ph)
            wc_next = 0
            for es in range(NE):
                for sc in range(4):
                    if es + 1 < NE:
                        wc_next = emit_wload_chunk(es + 1, sc)
                    st = emit_expert_part1(es, sc, wc)
                    yield
                    emit_expert_part2(es, sc, st)
                    yield
                wc = wc_next

        mg, eg = mixer_gen(), expert_gen()
        done_m = done_e = False
        while not (done_m and done_e):
            if not done_m:
                try:
                    next(mg)
                except StopIteration:
                    done_m = True
            if not done_e:
                try:
                    next(eg)
                except StopIteration:
                    done_e = True
        flush_pend()

        ALL_ACT = dA.n
        ALL_DN = dDN.n
        ALL_ATT = dAT.n
        PHASEA_T = tM.n
        W("sync", dA, ALL_ACT)
        W("sync", dDN, ALL_DN)
        W("tensor", sM, sM.n)
        W("tensor", vM, vM.n)
        phaseb_free = [0]
        lastw = {}
        asm_done = {}
        asm_fev = {}
        asm_out = {}
        for sc in range(4):
            for ei in range(NE):
                asl = ei % 2
                if ei >= 2:
                    W("sync", tM, lastw[(sc, ei - 2)])
                elif sc > 0:
                    W("sync", tM, lastw[(sc - 1, NE - 2 + ei)])
                else:
                    W("sync", tM, PHASEA_T)
                dAI[asl].bump(16)
                E("sync", lambda e, ei=ei, sc=sc, asl=asl:
                  e.dma_start(out=b_actin[:, asl * 4 * TC:(asl + 1) * 4 * TC],
                              in_=act_dram[ei, sc, :, :]).then_inc(dAI[asl].h, 16))
                AIN = dAI[asl].n
                dDNI[asl].bump(16)
                E("sync", lambda e, ei=ei, asl=asl:
                  e.dma_start(out=b_dnin[:, asl * 4096:(asl + 1) * 4096],
                              in_=dn_dram[ei, :, :]).then_inc(dDNI[asl].h, 16))
                DIN = dDNI[asl].n
                W("tensor", dAI[asl], AIN)
                W("tensor", dDNI[asl], DIN)
                if ei == 0:
                    W("tensor", sM, phaseb_free[0])
                for ic in range(4):
                    for ht in range(NDC):
                        oap = FB(ht)
                        tM.bump()
                        E("tensor", lambda e, ei=ei, ic=ic, ht=ht, asl=asl, o=oap:
                          e.matmul(out=o,
                                   lhsT=b_dnin[:, asl * 4096 + ic * 1024 + ht * 128:
                                               asl * 4096 + ic * 1024 + (ht + 1) * 128],
                                   rhs=b_actin[:, (asl * 4 + ic) * TC:(asl * 4 + ic + 1) * TC],
                                   start=(ei == 0 and ic == 0),
                                   stop=(ei == NE - 1 and ic == 3)).then_inc(tM.h, 1))
                lastw[(sc, ei)] = tM.n
            DOWN = tM.n
            W("sync", dAT, ALL_ATT)
            for ht in range(NDC):
                k = sc * NDC + ht
                sl = k % 2
                if k >= 2:
                    W("sync", vM, asm_done.get(k - 2, 0))
                dATI[sl].bump(32)
                E("sync", lambda e, ht=ht, sc=sc, sl=sl:
                  e.dma_start(out=s_ati[:, sl * TC:(sl + 1) * TC],
                              in_=attn_dram[ht, :, sc * TC:(sc + 1) * TC]).then_inc(dATI[sl].h, 16))
                E("sync", lambda e, ht=ht, sc=sc, sl=sl:
                  e.dma_start(out=s_xi[:, sl * TC:(sl + 1) * TC],
                              in_=x_fm[ht * 128:(ht + 1) * 128, sc * TC:(sc + 1) * TC]).then_inc(dATI[sl].h, 16))
                ATI = dATI[sl].n
                W("scalar", tM, DOWN)
                W("scalar", vM, asm_fev.get(k - 2, 0))
                oap = FB(ht)
                sM.bump()
                E("scalar", lambda e, ht=ht, sl=sl, o=oap:
                  e.copy(out=s_fev[:, sl * TC:(sl + 1) * TC], in_=o).then_inc(sM.h, 1))
                FEV = sM.n
                W("vector", sM, FEV)
                W("vector", dATI[sl], ATI)
                W("vector", dOut, asm_out.get(k - 2, 0))
                E("vector", lambda e, sl=sl:
                  e.tensor_add(s_o1[:, sl * TC:(sl + 1) * TC], s_fev[:, sl * TC:(sl + 1) * TC],
                               s_ati[:, sl * TC:(sl + 1) * TC]))
                vM.bump()
                E("vector", lambda e, sl=sl:
                  e.tensor_add(s_o1[:, sl * TC:(sl + 1) * TC], s_o1[:, sl * TC:(sl + 1) * TC],
                               s_xi[:, sl * TC:(sl + 1) * TC]).then_inc(vM.h, 1))
                OA = vM.n
                asm_done[k] = OA
                asm_fev[k] = OA
                W("sync", vM, OA)
                dOut.bump(16)
                asm_out[k] = dOut.n
                E("sync", lambda e, ht=ht, sc=sc, sl=sl:
                  e.dma_start(out=y_out[ht * 128:(ht + 1) * 128, sc * TC:(sc + 1) * TC],
                              in_=s_o1[:, sl * TC:(sl + 1) * TC]).then_inc(dOut.h, 16))
            phaseb_free[0] = sM.n

        W("sync", dOut, dOut.n)
        if dbg:
            dbv = [0]

            def dump(dst, srcap):
                dbv[0] += 16
                E("sync", lambda e, d=dst, s=srcap: e.dma_start(out=d, in_=s).then_inc(dOut.h, 16))
                dOut.n += 16
            dump(dbg_xn, xn[:, :])
            dump(dbg_ct, cT[:, :])
            dump(dbg_bb, b_B[:, :])
            dump(dbg_cc, b_C[:, :])
            dump(dbg_misc[:, 0:128], s_carry[:, :])
            dump(dbg_misc[:, 128:144], s_stat[:, :])
            dump(dbg_misc[:, 144:272], s_app[:, :])
            dump(dbg_misc[:, 272:400], s_papp[:, :])
            dump(dbg_misc[:, 400:912], s_y[:, :])
            dump(dbg_misc[:, 912:1424], s_y[:, :])
            dump(dbg_misc[:, 1424:1936], s_abar[:, 0:512])
            dump(dbg_misc[:, 1936:2448], s_wr[:, 0:512])
            dump(dbg_misc[:, 2448:2960], s_u[:, 0:512])
            dump(dbg_misc[:, 2960:3472], s_dl[:, 0:512])
            W("sync", dOut, dOut.n)

        with nc.Block() as block:
            @block.sync
            def _(e):
                for f in ops["sync"]:
                    f(e)

            @block.tensor
            def _(e):
                for f in ops["tensor"]:
                    f(e)

            @block.vector
            def _(e):
                for f in ops["vector"]:
                    f(e)

            @block.scalar
            def _(e):
                for f in ops["scalar"]:
                    f(e)

            @block.gpsimd
            def _(e):
                for f in ops["gpsimd"]:
                    f(e)

    return nc


_NC_CACHE = {}


def _packs(inputs):
    f = np.float32
    conv_w = np.asarray(inputs["L0_attn_conv_w"], f)
    conv_b = np.asarray(inputs["L0_attn_conv_b"], f)
    xproj = np.asarray(inputs["L0_attn_xproj_w"], f)
    dt_w = np.asarray(inputs["L0_attn_dt_w"], f)
    dt_b = np.asarray(inputs["L0_attn_dt_b"], f)
    a_log = np.asarray(inputs["L0_attn_a_log"], f)
    Dp = np.asarray(inputs["L0_attn_Dp"], f)
    gw = np.asarray(inputs["L0_ffn_gate_weight"], f)
    e_bias = np.asarray(inputs["L0_ffn_e_bias"], f)
    gp = np.asarray(inputs["L0_ffn_gate_proj"], f)
    up = np.asarray(inputs["L0_ffn_up_proj"], f)
    dn = np.asarray(inputs["L0_ffn_down_proj"], f)

    def fm_pack(v):
        return np.ascontiguousarray(v.reshape(NDC, 128).T)

    d = {}
    d["w33"] = np.ascontiguousarray(
        xproj.T.reshape(NDC, 128, 33).transpose(1, 0, 2).reshape(128, NDC * 33))
    d["gwp"] = np.ascontiguousarray(
        gw.T.reshape(NDC, 128, 16).transpose(1, 0, 2).reshape(128, NDC * 16))
    d["convw"] = np.ascontiguousarray(
        conv_w[:, 0, :].reshape(NDC, 128, 4).transpose(1, 2, 0).reshape(128, 4 * NDC))
    d["cb_pp"] = fm_pack(conv_b)
    d["dtw_rep"] = np.ascontiguousarray(np.tile(dt_w[:, 0][None, :], (128, 1)))
    d["dtb_rep"] = np.ascontiguousarray(np.tile(dt_b[None, :], (128, 1)))
    d["alog"] = np.ascontiguousarray(
        a_log.reshape(NDC, 128, N).transpose(1, 0, 2).reshape(128, NDC * N))
    d["dp_pp"] = fm_pack(Dp)
    d["ebias"] = np.ascontiguousarray(np.tile(e_bias[None, :], (128, 1)))
    d["ebr"] = np.ascontiguousarray(np.tile(e_bias[None, :], (128, NTT)))
    d["sel33"] = np.ascontiguousarray(np.repeat(np.eye(33, dtype=f), 128, axis=1))
    d["sel16"] = np.ascontiguousarray(np.repeat(np.eye(16, dtype=f), 128, axis=1))
    d["iden"] = np.eye(128, dtype=f)
    cst = np.zeros((128, 8), f)
    cst[:, C_NEG1] = -1.0
    cst[:, C_ONE] = 1.0
    cst[:, C_EPS] = 1e-5
    cst[:, C_HALF] = 0.5
    d["csts"] = cst
    d["gp_w"] = np.ascontiguousarray(
        gp.transpose(0, 2, 1).reshape(NE, NDC, 128, II).transpose(0, 2, 1, 3).reshape(NE, 128, NDC * II))
    d["up_w"] = np.ascontiguousarray(
        up.transpose(0, 2, 1).reshape(NE, NDC, 128, II).transpose(0, 2, 1, 3).reshape(NE, 128, NDC * II))
    d["dn_w"] = np.ascontiguousarray(
        dn.transpose(0, 2, 1).reshape(NE, 4, 128, D).transpose(0, 2, 1, 3).reshape(NE, 128, 4 * D))
    d["p_w33"] = d["w33"]
    d["p_convw"] = d["convw"]
    d["p_cb_pp"] = d["cb_pp"]
    d["p_dtw_rep"] = d["dtw_rep"]
    d["p_dtb_rep"] = d["dtb_rep"]
    d["p_alog"] = d["alog"]
    return d


def kernel(**inputs):
    inputs = dict(inputs)
    x = np.asarray(inputs["L0_attn_x"], np.float32)
    shared = _packs(inputs)
    zero_like = {k: np.zeros_like(v) for k, v in shared.items()
                 if k.startswith("p_")}

    in_maps = []
    for c in range(8):
        b, hf = c // 2, c % 2
        m = dict(shared)
        xp = np.ascontiguousarray(x[b, 0:T])
        if hf == 0:
            m.update(zero_like)
        m["hgate"] = np.full((128, 1), float(hf), np.float32)
        m["xp_tm"] = xp
        m["x_tm"] = np.ascontiguousarray(x[b, hf * T:(hf + 1) * T])
        m["x_fm"] = np.ascontiguousarray(x[b, hf * T:(hf + 1) * T].T)
        in_maps.append(m)

    dbg = bool(inputs.get("_dbg"))
    key = "nc_dbg" if dbg else "nc"
    if key not in _NC_CACHE:
        _NC_CACHE[key] = build_nc(dbg)
    nc = _NC_CACHE[key]
    res = run_bass_kernel_spmd(nc, in_maps, core_ids=list(range(8)))

    out = np.empty((B, L, D), np.float32)
    for c in range(8):
        b, hf = c // 2, c % 2
        out[b, hf * T:(hf + 1) * T, :] = np.asarray(res.results[c]["y_fm"]).T
    if dbg:
        return out, res.results
    return out


# version 18 (final)
# speedup vs baseline: 1.0507x; 1.0025x over previous
from contextlib import ExitStack
import numpy as np

import concourse.bass as bass
import concourse.mybir as mybir
from concourse.bass_utils import run_bass_kernel_spmd

F32 = mybir.dt.float32
BF16 = mybir.dt.bfloat16
AF = mybir.ActivationFunctionType
OP = mybir.AluOpType
AX = mybir.AxisListType

B, L, D, N, NE, II = 4, 4096, 1024, 16, 16, 512
T = 2048
NDC = 8
TC = 512
NTC = T // TC
NTT = T // 128
TP3 = T + 4
RING = 3
CLAG = RING
BIG = -1e30
MARK = 1e30

C_NEG1, C_ONE, C_EPS, C_HALF = 0, 1, 2, 3


class S:
    def __init__(self, nc, ctx, name):
        self.h = ctx.enter_context(nc.semaphore(name))
        self.n = 0

    def bump(self, k=1):
        self.n += k
        return self.n


def build_nc(dbg=False):
    DBG2 = [dbg]
    nc = bass.Bass("TRN2", target_bir_lowering=False, detect_race_conditions=False)
    ops = {k: [] for k in ("sync", "tensor", "vector", "scalar", "gpsimd")}

    def E(eng, fn):
        ops[eng].append(fn)

    def W(eng, sem, val):
        if val > 0:
            ops[eng].append(lambda e, s=sem.h, v=val: e.wait_ge(s, v))

    def din(name, shape, dt=F32):
        return nc.dram_tensor(name, shape, dt, kind="ExternalInput").ap()

    x_tm = din("x_tm", [T, D])
    xp_tm = din("xp_tm", [T, D])
    x_fm = din("x_fm", [D, T])
    w33 = din("w33", [128, NDC * 33])
    p_w33 = din("p_w33", [128, NDC * 33])
    gwp = din("gwp", [128, NDC * 16])
    convw = din("convw", [128, 4 * NDC])
    p_convw = din("p_convw", [128, 4 * NDC])
    cb_pp = din("cb_pp", [128, NDC])
    p_cb_pp = din("p_cb_pp", [128, NDC])
    dtw_rep = din("dtw_rep", [128, D])
    dtb_rep = din("dtb_rep", [128, D])
    p_dtw_rep = din("p_dtw_rep", [128, D])
    p_dtb_rep = din("p_dtb_rep", [128, D])
    alog = din("alog", [128, NDC * N])
    p_alog = din("p_alog", [128, NDC * N])
    dp_pp = din("dp_pp", [128, NDC])
    ebias = din("ebias", [128, NE])
    ebr = din("ebr", [128, NE * NTT])
    hgate = din("hgate", [128, 1])
    sel33 = din("sel33", [33, 33 * 128])
    sel16 = din("sel16", [16, 16 * 128])
    iden = din("iden", [128, 128])
    csts = din("csts", [128, 8])
    gp_w = din("gp_w", [NE, 128, NDC * II])
    up_w = din("up_w", [NE, 128, NDC * II])
    dn_w = din("dn_w", [NE, 128, 4 * D])

    KI = "ExternalOutput" if dbg else "Internal"
    act_dram = nc.dram_tensor("act_dram", [NE, NTC, 128, 4 * TC], BF16, kind=KI).ap()
    dn_dram = nc.dram_tensor("dn_dram", [NE, 128, 4 * D], BF16, kind=KI).ap()
    attn_dram = nc.dram_tensor("attn_dram", [NDC, 128, T], F32, kind=KI).ap()
    xc_dram = nc.dram_tensor("xc_dram", [2, NDC, 128, T], BF16, kind=KI).ap()
    dl_dram = nc.dram_tensor("dl_dram", [2, NTT, 128, NDC * 128], F32, kind=KI).ap()
    y_out = nc.dram_tensor("y_fm", [D, T], F32, kind="ExternalOutput").ap()
    if dbg:
        dbg_xn = nc.dram_tensor("dbg_xn", [128, NDC * TP3], BF16, kind="ExternalOutput").ap()
        dbg_misc = nc.dram_tensor("dbg_misc", [128, 4096], F32, kind="ExternalOutput").ap()
        dbg_ct = nc.dram_tensor("dbg_ct", [16, T], BF16, kind="ExternalOutput").ap()
        dbg_bb = nc.dram_tensor("dbg_bb", [128, N * TC], BF16, kind="ExternalOutput").ap()
        dbg_cc = nc.dram_tensor("dbg_cc", [128, N * TC], BF16, kind="ExternalOutput").ap()
        dbg_m2 = nc.dram_tensor("dbg_m2", [128, 4096], F32, kind="ExternalOutput").ap()
        dbg_m3 = nc.dram_tensor("dbg_m3", [128, 2048], BF16, kind="ExternalOutput").ap()

    with ExitStack() as ctx:
        def sb(name, shape, dt=F32):
            return ctx.enter_context(nc.sbuf_tensor(name, shape, dt))

        b_w33 = sb("b_w33", [128, NDC * 33], BF16)
        b_pw33 = sb("b_pw33", [128, NDC * 33], BF16)
        b_gwp = sb("b_gwp", [128, NDC * 16], BF16)
        s_convw = sb("s_convw", [128, 4 * NDC])
        s_pconvw = sb("s_pconvw", [128, 4 * NDC])
        b_cdiag = sb("b_cdiag", [128, 4 * 128], BF16)
        s_cb = sb("s_cb", [128, NDC])
        s_pcb = sb("s_pcb", [128, NDC])
        arf1 = sb("arf1", [128, 4096])
        s_dtwr = arf1[:, 0:1024]
        s_dtbr = arf1[:, 1024:2048]
        s_pdtwr = arf1[:, 2048:3072]
        s_pdtbr = arf1[:, 3072:4096]
        s_app = sb("s_app", [128, NDC * N])
        s_papp = sb("s_papp", [128, NDC * N])
        s_dp = sb("s_dp", [128, NDC])
        s_eb = sb("s_eb", [128, NE])
        b_sel33 = sb("b_sel33", [33, 33 * 128], BF16)
        b_sel16 = sb("b_sel16", [16, 16 * 128], BF16)
        b_iden = sb("b_iden", [128, 128], BF16)
        s_cst = sb("s_cst", [128, 8])
        s_hg = sb("s_hg", [128, 1])

        s_stat = sb("s_stat", [128, 16])
        s_zc = sb("s_zc", [128, 2])

        xn = sb("xn", [128, NDC * TP3], BF16)
        b_pjt = sb("b_pjt", [33, T], BF16)
        b_ppjt = sb("b_ppjt", [33, T], BF16)
        cT = sb("cT", [16, T], BF16)
        s_rt = sb("s_rt", [128, 112])
        s_cmb = sb("s_cmb", [128, NE])
        b_xc = sb("b_xc", [128, NDC * TC], BF16)

        b_B = sb("b_B", [128, N * TC], BF16)
        b_C = sb("b_C", [128, N * TC], BF16)
        TCX = TC + 4
        s_abar = sb("s_abar", [128, RING * TCX])
        s_wr = sb("s_wr", [128, RING * TCX])
        s_u = sb("s_u", [128, 2 * TC])
        s_dl = sb("s_dl", [128, 2 * TC])
        s_xnt = s_dl[:, 0:1024]
        b_xcl = sb("b_xcl", [128, 2 * TC], BF16)
        s_h8 = sb("s_h8", [128, 8 * TCX])
        s_xsq = s_wr[:, 0:1024]
        s_dltm = s_abar[:, 0:1024]
        s_dlfm = s_h8[:, 0:1024]
        s_pjf = s_abar[:, 1024:1090]
        s_carry = sb("s_carry", [128, 128])

        arf2 = sb("arf2", [128, 1536])
        s_xin = arf2[:, 0:1024]
        s_silu = arf2[:, 0:1024]
        s_m1 = arf2[:, 1024:1536]
        s_stage = sb("s_stage", [128, 2 * 1024])
        barena = sb("barena", [128, NDC * TP3], BF16)
        xnp_s = barena[:, :]
        b_gp0 = barena[:, 0 * 4096:1 * 4096]
        b_gp1 = barena[:, 1 * 4096:2 * 4096]
        b_up0 = barena[:, 2 * 4096:3 * 4096]
        b_up1 = barena[:, 3 * 4096:4 * 4096]
        b_actin = barena[:, 0:2 * 4 * TC]
        b_dnin = barena[:, 2 * 4 * TC:2 * 4 * TC + 2 * 4096]
        b_dnc = sb("b_dnc", [128, 1024], BF16)
        b_act = sb("b_act", [128, 4 * TC], BF16)

        s_y = arf1[:, 1024:1536]
        s_fev = arf1[:, 0:1024]
        s_ati = arf1[:, 1024:2048]
        s_xi = arf1[:, 2048:3072]
        s_o1 = arf1[:, 3072:4096]

        psF = ctx.enter_context(nc.psum_tensor("psF", [128, 8 * 512], F32))

        def FB(i, w=512):
            return psF[:, i * 512: i * 512 + w]

        sprc = s_h8[:, 4048:4112]

        def SP(n=1):
            for _ in range(n):
                E("vector", lambda e: e.tensor_scalar_mul(sprc, sprc, 1.0))

        dSm = S(nc, ctx, "dSm")
        dX = S(nc, ctx, "dX")
        dW = [S(nc, ctx, "dW0"), S(nc, ctx, "dW1")]
        dA = S(nc, ctx, "dA")
        dAI = [S(nc, ctx, "dAI0"), S(nc, ctx, "dAI1")]
        dDN = S(nc, ctx, "dDN")
        dDNI = [S(nc, ctx, "dDNI0"), S(nc, ctx, "dDNI1")]
        dAT = S(nc, ctx, "dAT")
        dATI = [S(nc, ctx, "dATI0"), S(nc, ctx, "dATI1")]
        dSpX = S(nc, ctx, "dSpX")
        dSpD = S(nc, ctx, "dSpD")
        dSlD = [S(nc, ctx, "dSlD0"), S(nc, ctx, "dSlD1")]
        dSlX = [S(nc, ctx, "dSlX0"), S(nc, ctx, "dSlX1")]
        dOut = S(nc, ctx, "dOut")
        tM = S(nc, ctx, "tM")
        vM = S(nc, ctx, "vM")
        sM = S(nc, ctx, "sM")
        vScan = S(nc, ctx, "vScan")
        sAb = S(nc, ctx, "sAb")
        sCp = S(nc, ctx, "sCp")
        gW = S(nc, ctx, "gW")
        vG = S(nc, ctx, "vG")
        sFr = S(nc, ctx, "sFr")

        s_iden_f = s_h8[:, 1024:1152]
        for dst, src in [(s_convw, convw), (s_pconvw, p_convw), (s_cb, cb_pp),
                         (s_pcb, p_cb_pp), (s_dtwr, dtw_rep), (s_dtbr, dtb_rep),
                         (s_pdtwr, p_dtw_rep), (s_pdtbr, p_dtb_rep),
                         (s_dp, dp_pp), (s_eb, ebias), (s_cst, csts),
                         (s_hg, hgate), (s_app, alog), (s_papp, p_alog),
                         (s_iden_f, iden), (s_h8[:, 3792:4048], ebr)]:
            dSm.bump(16)
            E("sync", lambda e, d=dst, s=src: e.dma_start(out=d[:, :], in_=s).then_inc(dSm.h, 16))
        SMALLS = DSM_ALL = dSm.n

        stage_k = [0]

        def stage_load(src_ap, pp, ww, cast_out_ap, inc_sems=()):
            k = stage_k[0]
            stage_k[0] += 1
            sl = k % 2
            W("sync", sFr, k - 1)
            dW[sl].bump(16)
            E("sync", lambda e, s=src_ap, sl=sl, p=pp, w=ww:
              e.dma_start(out=s_stage[:p, sl * 1024: sl * 1024 + w], in_=s).then_inc(dW[sl].h, 16))
            W("scalar", dW[sl], dW[sl].n)
            sFr.bump()
            E("scalar", lambda e, d=cast_out_ap, sl=sl, p=pp, w=ww, inc=tuple(inc_sems):
              e.copy(out=d, in_=s_stage[:p, sl * 1024: sl * 1024 + w]).then_inc(sFr.h, 1))

        for dst, src, pp, ftot in [(b_w33, w33, 128, NDC * 33), (b_pw33, p_w33, 128, NDC * 33),
                                   (b_gwp, gwp, 128, NDC * 16),
                                   (b_sel33, sel33, 33, 33 * 128), (b_sel16, sel16, 16, 16 * 128),
                                   (b_iden, iden, 128, 128)]:
            off = 0
            while off < ftot:
                w = min(1024, ftot - off)
                stage_load(src[:, off:off + w], pp, w, dst[:pp, off:off + w])
                off += w
        CASTS = sFr.n

        W("scalar", dSm, DSM_ALL)
        E("scalar", lambda e: e.activation(out=s_app[:, :], in_=s_app[:, :], func=AF.Exp))
        sM.bump(1)
        E("scalar", lambda e: e.activation(out=s_papp[:, :], in_=s_papp[:, :], func=AF.Exp).then_inc(sM.h, 1))
        APPX = sM.n
        W("vector", sM, APPX)
        E("vector", lambda e: e.tensor_scalar_mul(s_app[:, :], s_app[:, :], -1.0))
        E("vector", lambda e: e.tensor_scalar_mul(s_papp[:, :], s_papp[:, :], -1.0))
        E("vector", lambda e: e.memzero(s_stat[:, :]))
        E("vector", lambda e: e.memzero(s_rt[:, :]))
        for sl in range(RING):
            E("vector", lambda e, sl=sl: e.memzero(s_abar[:, sl * TCX:sl * TCX + 2]))
            E("vector", lambda e, sl=sl: e.memzero(s_wr[:, sl * TCX + 1 + TC:(sl + 1) * TCX]))
            E("vector", lambda e, sl=sl: e.memzero(s_abar[:, sl * TCX + 1 + TC:(sl + 1) * TCX]))
        E("vector", lambda e: e.memzero(sprc[:, :]))
        vM.bump()
        E("vector", lambda e: e.memzero(s_carry[:, :]).then_inc(vM.h, 1))
        APPN = vM.n



        ln_last_v2 = [0, 0]
        ln_sq_v = [0]
        ln_tr_t = [0]
        ln_tile = [0]

        def emit_ln(src_dram, dst_fm):
            for tt in range(NTT):
                sl = ln_tile[0] % 2
                ln_tile[0] += 1
                xslot = s_xin[:, 0:1024] if sl == 0 else s_stage[:, 0:1024]
                if ln_tile[0] == 2:
                    W("sync", sFr, CASTS)
                W("sync", vM, ln_last_v2[sl])
                dX.bump(16)
                E("sync", lambda e, s=src_dram, tt=tt, xslot=xslot:
                  e.dma_start(out=xslot,
                              in_=s[tt * 128:(tt + 1) * 128, :]).then_inc(dX.h, 16))
                xv = dX.n
                W("vector", dX, xv)
                W("scalar", dX, xv)
                W("scalar", vM, ln_sq_v[0])
                sM.bump()
                E("scalar", lambda e, xslot=xslot:
                  e.activation(out=s_xsq[:, :], in_=xslot,
                               func=AF.Square).then_inc(sM.h, 1))
                sqv = sM.n
                E("vector", lambda e, xslot=xslot:
                  e.reduce_sum(s_stat[:, 0:1], xslot, axis=AX.X))
                W("vector", sM, sqv)
                E("vector", lambda e: e.reduce_sum(s_stat[:, 1:2], s_xsq[:, :], axis=AX.X))
                E("vector", lambda e: e.tensor_scalar_mul(s_stat[:, 2:3], s_stat[:, 0:1], 1.0 / D))
                E("vector", lambda e: e.tensor_scalar_mul(s_stat[:, 3:4], s_stat[:, 1:2], 1.0 / D))
                E("vector", lambda e: e.tensor_mul(s_stat[:, 4:5], s_stat[:, 2:3], s_stat[:, 2:3]))
                SP()
                vM.bump()
                E("vector", lambda e: e.tensor_sub(s_stat[:, 5:6], s_stat[:, 3:4],
                                                   s_stat[:, 4:5]).then_inc(vM.h, 1))
                vv = vM.n
                ln_sq_v[0] = vv
                W("scalar", vM, vv)
                sM.bump()
                E("scalar", lambda e: e.activation(out=s_stat[:, 6:7], in_=s_stat[:, 5:6],
                                                   func=AF.Sqrt,
                                                   bias=s_cst[:, C_EPS:C_EPS + 1]).then_inc(sM.h, 1))
                sv = sM.n
                W("vector", sM, sv)
                E("vector", lambda e: e.reciprocal(s_stat[:, 7:8], s_stat[:, 6:7]))
                SP()
                W("vector", tM, ln_tr_t[0])
                vM.bump()
                E("vector", lambda e, xslot=xslot:
                  e.tensor_scalar(out=s_xnt[:, :], in0=xslot,
                                  scalar1=s_stat[:, 2:3], scalar2=s_stat[:, 7:8],
                                  op0=OP.subtract, op1=OP.mult).then_inc(vM.h, 1))
                vv = vM.n
                ln_last_v2[sl] = vv
                W("tensor", vM, vv)
                for dc in range(NDC):
                    tM.bump()
                    E("tensor", lambda e, dc=dc:
                      e.transpose(out=psF[:, 5 * 512 + dc * 128: 5 * 512 + (dc + 1) * 128],
                                  in_=s_xnt[:, dc * 128:(dc + 1) * 128],
                                  identity=s_iden_f[:, :]).then_inc(tM.h, 1))
                W("scalar", tM, tM.n)
                sM.bump(NDC)
                for dc in range(NDC):
                    last = dc == NDC - 1
                    E("scalar", lambda e, dc=dc, tt=tt, d=dst_fm, last=last:
                      (e.copy(out=d[:, dc * TP3 + 4 + tt * 128: dc * TP3 + 4 + (tt + 1) * 128],
                              in_=psF[:, 5 * 512 + dc * 128: 5 * 512 + (dc + 1) * 128]).then_inc(sM.h, NDC) if last else
                       e.copy(out=d[:, dc * TP3 + 4 + tt * 128: dc * TP3 + 4 + (tt + 1) * 128],
                              in_=psF[:, 5 * 512 + dc * 128: 5 * 512 + (dc + 1) * 128])))
                W("tensor", sM, sM.n)
                ln_tr_t[0] = tM.n

        W("tensor", sFr, CASTS)
        emit_ln(x_tm, xn)
        emit_ln(xp_tm, xnp_s)
        LN_STAGE_V = vM.n
        XN_S = sM.n
        W("scalar", dSm, DSM_ALL)
        sM.bump()
        for dc in range(NDC):
            last = dc == NDC - 1
            E("scalar", lambda e, dc=dc, last=last:
              (e.activation(out=xn[:, dc * TP3: dc * TP3 + 4],
                            in_=xnp_s[:, dc * TP3 + T: dc * TP3 + T + 4],
                            func=AF.Copy, scale=s_hg[:, 0:1]).then_inc(sM.h, 1) if last else
               e.activation(out=xn[:, dc * TP3: dc * TP3 + 4],
                            in_=xnp_s[:, dc * TP3 + T: dc * TP3 + T + 4],
                            func=AF.Copy, scale=s_hg[:, 0:1])))
        W("vector", sM, sM.n)
        for dc in range(NDC):
            E("vector", lambda e, dc=dc: e.memzero(xnp_s[:, dc * TP3: dc * TP3 + 4]))
        vM.bump()
        E("vector", lambda e: e.engine_nop().then_inc(vM.h, 1))
        XN_V = vM.n



        xc_spill_v = [0]
        conv_mm_v = [0]
        dltr_t = [0]
        dlsp_v = [0]

        def emit_convproj(pfx):
            src_xn = xnp_s if pfx else xn
            cw = s_pconvw if pfx else s_convw
            cb = s_pcb if pfx else s_cb
            w33b = b_pw33 if pfx else b_w33
            pjt = b_ppjt if pfx else b_pjt
            for tc in range(NTC):
                W("scalar", dSpX, xc_spill_v[0])
                W("tensor", sM, sM.n)
                for dc in range(NDC):
                    W("vector", tM, conv_mm_v[0])
                    vM.bump()

                    def mkdiag(e, dc=dc):
                        for k in range(4):
                            e.tensor_scalar(out=b_cdiag[:, k * 128:(k + 1) * 128],
                                            in0=b_iden[:, :],
                                            scalar1=cw[:, k * NDC + dc: k * NDC + dc + 1],
                                            scalar2=None, op0=OP.mult)
                        e.engine_nop().then_inc(vM.h, 1)
                    E("vector", mkdiag)
                    dgv = vM.n
                    W("tensor", vM, dgv)
                    for k in range(4):
                        tM.bump()
                        E("tensor", lambda e, dc=dc, k=k, tc=tc, s=src_xn:
                          e.matmul(out=FB(5),
                                   lhsT=b_cdiag[:, k * 128:(k + 1) * 128],
                                   rhs=s[:, dc * TP3 + tc * TC + k + 1: dc * TP3 + tc * TC + k + 1 + TC],
                                   start=(k == 0), stop=(k == 3)).then_inc(tM.h, 1))
                    conv_mm_v[0] = tM.n
                    W("scalar", tM, tM.n)
                    sM.bump()
                    E("scalar", lambda e, dc=dc, cb=cb:
                      e.activation(out=b_xc[:, dc * TC:(dc + 1) * TC], in_=FB(5),
                                   func=AF.Silu, bias=cb[:, dc:dc + 1]).then_inc(sM.h, 1))
                    W("tensor", sM, sM.n)
                XCD = sM.n
                W("sync", sM, XCD)
                dSpX.bump(16)
                xc_spill_v[0] = dSpX.n
                E("sync", lambda e, pfx=pfx, tc=tc:
                  e.dma_start(out=xc_dram[1 if pfx else 0, :, :, tc * TC:(tc + 1) * TC]
                              .rearrange("c p t -> p c t"),
                              in_=b_xc[:, :].rearrange("p (c t) -> p c t", c=NDC)).then_inc(dSpX.h, 16))
                for st in range(TC // 128):
                    tt = tc * (TC // 128) + st
                    for dc in range(NDC):
                        tM.bump()
                        E("tensor", lambda e, dc=dc, st=st, w=w33b:
                          e.matmul(out=FB(6, 33),
                                   lhsT=b_xc[:, dc * TC + st * 128: dc * TC + (st + 1) * 128],
                                   rhs=w[:, dc * 33:(dc + 1) * 33],
                                   start=(dc == 0), stop=(dc == NDC - 1)).then_inc(tM.h, 1))
                    PJ = tM.n
                    W("scalar", tM, PJ)
                    sM.bump(2)
                    E("scalar", lambda e, st=st:
                      e.copy(out=s_pjf[:, (st % 2) * 33:(st % 2) * 33 + 33], in_=FB(6, 33)))
                    E("scalar", lambda e, st=st:
                      e.copy(out=s_zc[:, st % 2: st % 2 + 1], in_=FB(6, 33)[:, 32:33]).then_inc(sM.h, 2))
                    ZV = sM.n
                    W("vector", sM, ZV)
                    W("vector", tM, dltr_t[0])
                    vM.bump()
                    E("vector", lambda e, st=st, pfx=pfx:
                      e.scalar_tensor_tensor(out=s_dltm[:, :],
                                             in0=(s_pdtwr if pfx else s_dtwr)[:, :],
                                             scalar=s_zc[:, st % 2: st % 2 + 1],
                                             in1=(s_pdtbr if pfx else s_dtbr)[:, :],
                                             op0=OP.mult, op1=OP.add).then_inc(vM.h, 1))
                    ZETA = vM.n
                    W("scalar", vM, ZETA)
                    sM.bump(3)
                    E("scalar", lambda e: e.activation(out=s_xsq[:, :], in_=s_dltm[:, :], func=AF.Abs))
                    E("scalar", lambda e: e.activation(out=s_xsq[:, :], in_=s_xsq[:, :], func=AF.Exp,
                                                       scale=s_cst[:, C_NEG1:C_NEG1 + 1]))
                    E("scalar", lambda e: e.activation(out=s_xsq[:, :], in_=s_xsq[:, :], func=AF.Ln,
                                                       bias=s_cst[:, C_ONE:C_ONE + 1]).then_inc(sM.h, 3))
                    SPC = sM.n
                    W("vector", sM, SPC)
                    vM.bump()
                    E("vector", lambda e:
                      e.scalar_tensor_tensor(out=s_dltm[:, :], in0=s_dltm[:, :], scalar=0.0,
                                             in1=s_xsq[:, :], op0=OP.max, op1=OP.add).then_inc(vM.h, 1))
                    DLT = vM.n
                    W("tensor", vM, DLT)
                    W("scalar", dSpD, dlsp_v[0])
                    for dc in range(NDC):
                        tM.bump()
                        E("tensor", lambda e, dc=dc:
                          e.transpose(out=psF[:, 4 * 512 + 128:4 * 512 + 256],
                                      in_=s_dltm[:, dc * 128:(dc + 1) * 128],
                                      identity=s_iden_f[:, :]).then_inc(tM.h, 1))
                        W("scalar", tM, tM.n)
                        sM.bump()
                        E("scalar", lambda e, dc=dc:
                          e.copy(out=s_dlfm[:, dc * 128:(dc + 1) * 128],
                                 in_=psF[:, 4 * 512 + 128:4 * 512 + 256]).then_inc(sM.h, 1))
                        W("tensor", sM, sM.n)
                    dltr_t[0] = tM.n
                    DFM = sM.n
                    W("sync", sM, DFM)
                    dSpD.bump(16)
                    dlsp_v[0] = dSpD.n
                    E("sync", lambda e, pfx=pfx, tt=tt:
                      e.dma_start(out=dl_dram[1 if pfx else 0, tt, :, :],
                                  in_=s_dlfm[:, :]).then_inc(dSpD.h, 16))
                    if False and DBG2[0] and not pfx and tt == 0:
                        DBG2[0] = False
                        for dof, srcap in [(0, s_dlfm[:, :]), (1024, s_dltm[:, :]),
                                           (2048, s_xsq[:, :]),
                                           (3072, arf1[:, 0:1024])]:
                            dOut.bump(16)
                            E("sync", lambda e, dof=dof, s=srcap:
                              e.dma_start(out=dbg_m2[:, dof:dof + s.shape[1]],
                                          in_=s).then_inc(dOut.h, 16))
                    tM.bump()
                    E("tensor", lambda e, st=st:
                      e.transpose(out=FB(4, 128)[0:33, :], in_=s_pjf[:, (st % 2) * 33:(st % 2) * 33 + 33],
                                  identity=s_iden_f[:, :]).then_inc(tM.h, 1))
                    W("scalar", tM, tM.n)
                    sM.bump()
                    E("scalar", lambda e, tt=tt, p=pjt:
                      e.copy(out=p[:, tt * 128:(tt + 1) * 128], in_=FB(4, 128)[0:33, :]).then_inc(sM.h, 1))
                    W("tensor", sM, sM.n)
            return sM.n

        W("tensor", dSm, DSM_ALL)
        W("tensor", vM, XN_V)
        if dbg:
            W("sync", vM, XN_V)
            for dof, srcap in [(0, xnp_s[:, 0:1024]), (1024, xnp_s[:, 4 * TP3:4 * TP3 + 1024])]:
                dOut.bump(16)
                E("sync", lambda e, dof=dof, s=srcap:
                  e.dma_start(out=dbg_m3[:, dof:dof + 1024], in_=s).then_inc(dOut.h, 16))

        emit_convproj(False)
        CP_MAIN = sM.n
        emit_convproj(True)
        CP_PFX = sM.n
        ALL_SPX = dSpX.n
        ALL_SPD = dSpD.n

        rt_sig = {}
        scv_all = s_h8[:, 1152:1408]
        s4c_all = s_h8[:, 1408:1664]
        gsc_all = s_h8[:, 1664:1728]
        grkA = s_h8[:, 1728:1792]
        grkB = s_h8[:, 1792:1856]
        p6 = s_h8[:, 1856:2240]
        msk_all = s_h8[:, 1856:2112]
        rnkA = s_h8[:, 2240:2496]
        rnkB = s_h8[:, 2496:2752]
        mrA = s_h8[:, 2752:3008]
        mrB = s_h8[:, 3008:3264]
        wsl_all = s_h8[:, 3264:3520]
        wsm_all = s_h8[:, 3520:3536]
        cmb_all = s_h8[:, 3536:3792]
        ebr_s = s_h8[:, 3792:4048]

        for tt in range(NTT):
            bank = tt % 2
            if tt == 0:
                W("tensor", sFr, CASTS)
                W("tensor", sM, sM.n)
            W("tensor", sM, rt_sig.get(tt - 2, 0))
            for dc in range(NDC):
                tM.bump()
                E("tensor", lambda e, tt=tt, dc=dc, bank=bank:
                  e.matmul(out=FB(4 + bank, 16),
                           lhsT=xn[:, dc * TP3 + 4 + tt * 128: dc * TP3 + 4 + (tt + 1) * 128],
                           rhs=b_gwp[:, dc * 16:(dc + 1) * 16],
                           start=(dc == 0), stop=(dc == NDC - 1)).then_inc(tM.h, 1))
            W("scalar", tM, tM.n)
            sM.bump()
            E("scalar", lambda e, tt=tt, bank=bank:
              e.activation(out=scv_all[:, tt * 16:(tt + 1) * 16], in_=FB(4 + bank, 16),
                           func=AF.Sigmoid).then_inc(sM.h, 1))
            rt_sig[tt] = sM.n
        SIG_ALL = sM.n

        W("vector", sM, SIG_ALL)
        W("vector", dSm, DSM_ALL)
        pairs = [(0, 1), (0, 2), (0, 3), (1, 2), (1, 3), (2, 3)]
        g4 = s4c_all.rearrange("p (tg i) -> p tg i", i=4)
        E("vector", lambda e: e.tensor_add(s4c_all, scv_all, ebr_s))
        SP()
        for k, (i, j) in enumerate(pairs):
            E("vector", lambda e, k=k, i=i, j=j:
              e.tensor_add(p6[:, k * 64:(k + 1) * 64], g4[:, :, i], g4[:, :, j]))
        E("vector", lambda e: e.tensor_max(gsc_all, p6[:, 0:64], p6[:, 64:128]))
        E("vector", lambda e: e.tensor_max(grkB, p6[:, 128:192], p6[:, 192:256]))
        E("vector", lambda e: e.tensor_max(grkA, p6[:, 256:320], p6[:, 320:384]))
        E("vector", lambda e: e.tensor_max(gsc_all, gsc_all, grkB))
        SP()
        E("vector", lambda e: e.tensor_max(gsc_all, gsc_all, grkA))
        SP()
        E("vector", lambda e: e.memset(grkA, 0.0))
        E("vector", lambda e: e.memset(grkB, 0.0))
        gv = gsc_all.rearrange("p (t g) -> p t g", g=4)
        gA = grkA.rearrange("p (t g) -> p t g", g=4)
        gB = grkB.rearrange("p (t g) -> p t g", g=4)
        tA4 = mrA[:, 0:64].rearrange("p (t g) -> p t g", g=4)
        tB4 = mrB[:, 0:64].rearrange("p (t g) -> p t g", g=4)
        for s in range(1, 4):
            E("vector", lambda e, s=s: e.tensor_tensor(out=tA4[:, :, 0:4 - s], in0=gv[:, :, 0:4 - s],
                                                       in1=gv[:, :, s:4], op=OP.is_ge))
            E("vector", lambda e, s=s: e.tensor_tensor(out=tB4[:, :, 0:4 - s], in0=gv[:, :, s:4],
                                                       in1=gv[:, :, 0:4 - s], op=OP.is_gt))
            E("vector", lambda e, s=s: e.tensor_tensor(out=gA[:, :, s:4], in0=gA[:, :, s:4],
                                                       in1=tA4[:, :, 0:4 - s], op=OP.add))
            E("vector", lambda e, s=s: e.tensor_tensor(out=gB[:, :, 0:4 - s], in0=gB[:, :, 0:4 - s],
                                                       in1=tB4[:, :, 0:4 - s], op=OP.add))
        SP()
        E("vector", lambda e: e.tensor_add(grkA, grkA, grkB))
        SP(2)
        E("vector", lambda e: e.tensor_single_scalar(out=grkA, in_=grkA, scalar=2.0, op=OP.is_lt))
        SP(2)
        for i in range(4):
            E("vector", lambda e, i=i: e.tensor_tensor(out=g4[:, :, i], in0=g4[:, :, i],
                                                       in1=grkA, op=OP.mult))
        E("vector", lambda e: e.memset(rnkA, 0.0))
        E("vector", lambda e: e.memset(rnkB, 0.0))
        sv = s4c_all.rearrange("p (t j) -> p t j", j=16)
        vA = rnkA.rearrange("p (t j) -> p t j", j=16)
        vB = rnkB.rearrange("p (t j) -> p t j", j=16)
        tAe = mrA.rearrange("p (t j) -> p t j", j=16)
        tBe = mrB.rearrange("p (t j) -> p t j", j=16)
        for s in range(1, 16):
            w_ = 16 - s
            E("vector", lambda e, s=s, w_=w_: e.tensor_tensor(out=tAe[:, :, 0:w_], in0=sv[:, :, 0:w_],
                                                              in1=sv[:, :, s:16], op=OP.is_ge))
            E("vector", lambda e, s=s, w_=w_: e.tensor_tensor(out=tBe[:, :, 0:w_], in0=sv[:, :, s:16],
                                                              in1=sv[:, :, 0:w_], op=OP.is_gt))
            E("vector", lambda e, s=s, w_=w_: e.tensor_tensor(out=vA[:, :, s:16], in0=vA[:, :, s:16],
                                                              in1=tAe[:, :, 0:w_], op=OP.add))
            E("vector", lambda e, s=s, w_=w_: e.tensor_tensor(out=vB[:, :, 0:w_], in0=vB[:, :, 0:w_],
                                                              in1=tBe[:, :, 0:w_], op=OP.add))
        SP()
        E("vector", lambda e: e.tensor_add(rnkA, rnkA, rnkB))
        SP(2)
        E("vector", lambda e: e.tensor_single_scalar(out=msk_all, in_=rnkA, scalar=4.0, op=OP.is_lt))
        SP(2)
        E("vector", lambda e: e.tensor_mul(wsl_all, scv_all, msk_all))
        SP(2)
        wv = wsl_all.rearrange("p (t j) -> p t j", j=16)
        t8 = mrA[:, 0:128].rearrange("p (t j) -> p t j", j=8)
        t4 = mrB[:, 0:64].rearrange("p (t j) -> p t j", j=4)
        t2 = mrA[:, 128:160].rearrange("p (t j) -> p t j", j=2)
        E("vector", lambda e: e.tensor_tensor(out=t8[:, :, :], in0=wv[:, :, 0:8],
                                              in1=wv[:, :, 8:16], op=OP.add))
        SP()
        E("vector", lambda e: e.tensor_tensor(out=t4[:, :, :], in0=t8[:, :, 0:4],
                                              in1=t8[:, :, 4:8], op=OP.add))
        SP()
        E("vector", lambda e: e.tensor_tensor(out=t2[:, :, :], in0=t4[:, :, 0:2],
                                              in1=t4[:, :, 2:4], op=OP.add))
        SP()
        E("vector", lambda e: e.tensor_tensor(out=wsm_all, in0=t2[:, :, 0],
                                              in1=t2[:, :, 1], op=OP.add))
        SP()
        E("vector", lambda e: e.tensor_scalar_add(wsm_all, wsm_all, 1e-20))
        SP(2)
        E("vector", lambda e: e.reciprocal(wsm_all, wsm_all))
        SP(2)
        cv = cmb_all.rearrange("p (t j) -> p t j", j=16)
        vM.bump()
        for i in range(16):
            last = i == 15
            E("vector", lambda e, i=i, last=last:
              (e.tensor_tensor(out=cv[:, :, i], in0=wv[:, :, i], in1=wsm_all,
                               op=OP.mult).then_inc(vM.h, 1) if last else
               e.tensor_tensor(out=cv[:, :, i], in0=wv[:, :, i], in1=wsm_all,
                               op=OP.mult)))
        CMB_ALL = vM.n
        if dbg:
            W("sync", vM, CMB_ALL)
            dOut.bump(16)
            E("sync", lambda e: e.dma_start(out=dbg_m2[:, 3584:3712],
                                            in_=s_h8[:, 3536:3664]).then_inc(dOut.h, 16))
            W("vector", dOut, dOut.n)

        W("tensor", vM, CMB_ALL)
        rt_ev = [0]
        for tt in range(NTT):
            W("tensor", sM, rt_ev[0])
            tM.bump()
            E("tensor", lambda e, tt=tt:
              e.transpose(out=psF[:, 4 * 512 + 256:4 * 512 + 384][0:16, :],
                          in_=cmb_all[:, tt * 16:(tt + 1) * 16],
                          identity=s_iden_f[:, :]).then_inc(tM.h, 1))
            W("scalar", tM, tM.n)
            sM.bump()
            E("scalar", lambda e, tt=tt:
              e.copy(out=cT[:, tt * 128:(tt + 1) * 128],
                     in_=psF[:, 4 * 512 + 256:4 * 512 + 384][0:16, :]).then_inc(sM.h, 1))
            rt_ev[0] = sM.n
        W("tensor", sM, sM.n)
        ROUTER_S = sM.n

        abk = [0]
        ubk = [0]
        guf_g = [0, 0]
        guf_u = [0, 0]
        crep_free = [0]
        attn_k = [0]

        def emit_bc(tc, pfx):
            W("scalar", gW, gW.n)
            W("scalar", vM, vM.n)
            pjt = b_ppjt if pfx else b_pjt
            targets = [("B", n) for n in range(N)]
            if not pfx:
                targets += [("C", n) for n in range(N)]
            for gi, (kind, n) in enumerate(targets):
                bank = 5 + (gi % 2)
                row = n if kind == "B" else 16 + n
                tM.bump()
                E("tensor", lambda e, row=row, bank=bank, tc=tc, p=pjt:
                  e.matmul(out=FB(bank), lhsT=b_sel33[:, row * 128:(row + 1) * 128],
                           rhs=p[:, tc * TC:(tc + 1) * TC],
                           start=True, stop=True).then_inc(tM.h, 1))
                W("scalar", tM, tM.n)
                dst = b_B if kind == "B" else b_C
                sM.bump()
                E("scalar", lambda e, d=dst, n=n, bank=bank:
                  e.copy(out=d[:, n * TC:(n + 1) * TC], in_=FB(bank)).then_inc(sM.h, 1))
                W("tensor", sM, sM.n)
            return sM.n

        u_done = {}
        ab_done = {}
        scan_col = {}
        pend_sync = []

        def flush_pend():
            for fn in pend_sync:
                fn()
            pend_sync.clear()

        def emit_mixer_dc(tc, dc, pfx, bc_s):
            unit = abk[0] // N
            ring = unit % 2
            app = s_papp if pfx else s_app
            if unit == 0:
                W("sync", dSpX, ALL_SPX)
                W("sync", dSpD, ALL_SPD)
            if unit >= 2:
                W("sync", sAb, ab_done[unit - 2])
                W("sync", vG, u_done[unit - 2])
            dSlD[ring].bump(16)
            E("sync", lambda e, pfx=pfx, tc=tc, dc=dc, ring=ring:
              e.dma_start(out=s_dl[:, ring * TC:(ring + 1) * TC]
                          .rearrange("p (a t) -> p a t", a=4),
                          in_=dl_dram[1 if pfx else 0, tc * 4:(tc + 1) * 4, :, dc * 128:(dc + 1) * 128]
                          .rearrange("a p t -> p a t")).then_inc(dSlD[ring].h, 16))
            dSlX[ring].bump(16)
            E("sync", lambda e, pfx=pfx, tc=tc, dc=dc, ring=ring:
              e.dma_start(out=b_xcl[:, ring * TC:(ring + 1) * TC],
                          in_=xc_dram[1 if pfx else 0, dc, :, tc * TC:(tc + 1) * TC]).then_inc(dSlX[ring].h, 16))
            DLX = (dSlD[ring].n, dSlX[ring].n)
            flush_pend()
            W("vector", dSlD[ring], DLX[0])
            W("vector", dSlX[ring], DLX[1])
            if unit >= 2:
                W("vector", gW, (unit - 1) * N)
            vG.bump()
            E("vector", lambda e, ring=ring:
              e.tensor_mul(s_u[:, ring * TC:(ring + 1) * TC],
                           s_dl[:, ring * TC:(ring + 1) * TC],
                           b_xcl[:, ring * TC:(ring + 1) * TC]).then_inc(vG.h, 1))
            UV = vG.n
            u_done[unit] = UV
            u0 = abk[0]
            W("scalar", dSlD[ring], DLX[0])
            if unit == 0:
                W("scalar", vM, APPN)

            def sc_iters(n0, n1, copy_from=CLAG):
                for n in range(n0, n1):
                    k = u0 + n
                    W("scalar", vScan, k - (RING - 1))
                    if n >= copy_from:
                        kc = k - CLAG
                        if kc == u0 + 7:
                            sCp.bump()
                            E("scalar", lambda e, kc=kc, col=scan_col[kc]:
                              e.copy(out=s_carry[:, col:col + 1],
                                     in_=s_h8[:, (kc % 8) * TCX + TC:(kc % 8) * TCX + TC + 1])
                              .then_inc(sCp.h, 1))
                        else:
                            E("scalar", lambda e, kc=kc, col=scan_col[kc]:
                              e.copy(out=s_carry[:, col:col + 1],
                                     in_=s_h8[:, (kc % 8) * TCX + TC:(kc % 8) * TCX + TC + 1]))
                    E("scalar", lambda e, k=k, ring=ring, dc=dc, n=n, a=app:
                      e.activation(out=s_abar[:, (k % RING) * TCX + 1:(k % RING) * TCX + 1 + TC],
                                   in_=s_dl[:, ring * TC:(ring + 1) * TC], func=AF.Exp,
                                   scale=a[:, dc * N + n: dc * N + n + 1]))
                    sAb.bump()
                    E("scalar", lambda e, k=k, dc=dc, n=n:
                      e.copy(out=s_wr[:, (k % RING) * TCX:(k % RING) * TCX + 1],
                             in_=s_carry[:, dc * N + n: dc * N + n + 1]).then_inc(sAb.h, 1))
                    scan_col[k] = dc * N + n

            def vec_half(half):
                for j in range(8):
                    n = half * 8 + j
                    k = u0 + n
                    W("vector", sAb, k + 1)
                    W("vector", gW, k + 1)
                    vScan.bump()
                    E("vector", lambda e, k=k, j=j:
                      e.tensor_tensor_scan(out=s_h8[:, j * TCX:j * TCX + 1 + TC],
                                           data0=s_abar[:, (k % RING) * TCX:(k % RING) * TCX + 1 + TC],
                                           data1=s_wr[:, (k % RING) * TCX:(k % RING) * TCX + 1 + TC],
                                           initial=0.0,
                                           op0=OP.mult, op1=OP.add).then_inc(vScan.h, 1))
                if pfx:
                    return
                if half == 0:
                    W("vector", sCp, 2 * unit + 1)
                else:
                    W("vector", sCp, 2 * unit + 2)
                h3 = s_h8[:, :].rearrange("p (n t) -> p n t", n=8)[:, :, 1:1 + TC]
                c3 = b_C[:, half * 8 * TC:(half + 1) * 8 * TC].rearrange("p (n t) -> p n t", n=8)
                E("vector", lambda e, h3=h3, c3=c3:
                  e.tensor_tensor(out=h3, in0=h3, in1=c3, op=OP.mult))
                h4a = h3[:, 0:4, :]
                h4b = h3[:, 4:8, :]
                E("vector", lambda e, h4a=h4a, h4b=h4b:
                  e.tensor_tensor(out=h4a, in0=h4a, in1=h4b, op=OP.add))
                h2a = h3[:, 0:2, :]
                h2b = h3[:, 2:4, :]
                E("vector", lambda e, h2a=h2a, h2b=h2b:
                  e.tensor_tensor(out=h2a, in0=h2a, in1=h2b, op=OP.add))
                h1a = s_h8[:, 1:1 + TC]
                h1b = s_h8[:, TCX + 1:TCX + 1 + TC]
                if half == 0:
                    W("vector", dAT, attn_k[0] * 16)
                    E("vector", lambda e, h1a=h1a, h1b=h1b:
                      e.tensor_add(s_y[:, :], h1a, h1b))
                else:
                    E("vector", lambda e, h1a=h1a, h1b=h1b:
                      e.tensor_add(s_y[:, :], s_y[:, :], h1a))
                    E("vector", lambda e, h1b=h1b:
                      e.tensor_add(s_y[:, :], s_y[:, :], h1b))

            sc_iters(0, 11)
            W("gpsimd", vG, UV)
            W("gpsimd", sM, bc_s)
            for n in range(N):
                k = u0 + n
                W("gpsimd", vScan, k - (RING - 1))
                gW.bump()
                E("gpsimd", lambda e, k=k, n=n, ring=ring:
                  e.tensor_tensor(out=s_wr[:, (k % RING) * TCX + 1:(k % RING) * TCX + 1 + TC],
                                  in0=s_u[:, ring * TC:(ring + 1) * TC],
                                  in1=b_B[:, n * TC:(n + 1) * TC], op=OP.mult).then_inc(gW.h, 1))
            vec_half(0)
            yield
            sc_iters(11, 16)
            for kc in range(u0 + N - CLAG, u0 + N):
                W("scalar", vScan, kc + 1)
                if kc == u0 + N - 1:
                    sCp.bump()
                    E("scalar", lambda e, kc=kc, col=scan_col[kc]:
                      e.copy(out=s_carry[:, col:col + 1],
                             in_=s_h8[:, (kc % 8) * TCX + TC:(kc % 8) * TCX + TC + 1])
                      .then_inc(sCp.h, 1))
                else:
                    E("scalar", lambda e, kc=kc, col=scan_col[kc]:
                      e.copy(out=s_carry[:, col:col + 1],
                             in_=s_h8[:, (kc % 8) * TCX + TC:(kc % 8) * TCX + TC + 1]))
            ab_done[unit] = sAb.n
            vec_half(1)
            abk[0] += N
            if dbg and DBG_K9[0] and abk[0] >= 16:
                DBG_K9[0] = False
                W("sync", vScan, 9)
                for dof, srcap in [(0, s_wr[:, 2 * TCX:2 * TCX + 16]),
                                   (16, s_abar[:, 2 * TCX:2 * TCX + 16]),
                                   (32, s_h8[:, 0:16]),
                                   (48, s_carry[:, 0:16]),
                                   (64, s_u[:, 0:16]),
                                   (80, s_dl[:, 0:16]),
                                   (96, s_wr[:, 2 * TCX + 496:2 * TCX + 516])]:
                    dOut.bump(16)
                    E("sync", lambda e, dof=dof, s=srcap:
                      e.dma_start(out=dbg_m2[:, 3712 + dof:3712 + dof + s.shape[1]],
                                  in_=s).then_inc(dOut.h, 16))
                W("vector", dOut, dOut.n)
            if DBG_U1[0]:
                DBG_U1[0] = False
                W("sync", vScan, N)
                for dof, srcap in [(0, s_carry[:, :]), (128, s_h8[:, 0:516]),
                                   (644, s_wr[:, 0:1548]), (2192, s_abar[:, 0:1548])]:
                    dOut.bump(16)
                    E("sync", lambda e, dof=dof, s=srcap:
                      e.dma_start(out=dbg_m2[:, dof:dof + s.shape[1]], in_=s).then_inc(dOut.h, 16))
            ak = attn_k[0]
            attn_k[0] += 1
            vM.bump()
            E("vector", lambda e, dc=dc, tc=tc:
              e.scalar_tensor_tensor(out=s_y[:, :],
                                     in0=xn[:, dc * TP3 + 4 + tc * TC: dc * TP3 + 4 + (tc + 1) * TC],
                                     scalar=s_dp[:, dc:dc + 1], in1=s_y[:, :],
                                     op0=OP.mult, op1=OP.add).then_inc(vM.h, 1))
            attn_v = vM.n
            dAT.bump(16)

            def _attn_spill(dc=dc, tc=tc, attn_v=attn_v):
                W("sync", vM, attn_v)
                E("sync", lambda e, dc=dc, tc=tc:
                  e.dma_start(out=attn_dram[dc, :, tc * TC:(tc + 1) * TC],
                              in_=s_y[:, :]).then_inc(dAT.h, 16))
            pend_sync.append(_attn_spill)

        wload_k = [0]

        def emit_wload_item(ei, item):
            slot = ei % 2
            if ei == 0 and item == 0:
                W("sync", tM, TM_PROLOGUE[0])
                W("sync", vM, LN_STAGE_V)
            dsts = {0: (b_gp0, b_gp1), 1: (b_up0, b_up1)}
            k = wload_k[0]
            wload_k[0] += 1
            ssl = k % 2
            W("sync", sFr, max(CASTS, CASTS + k - 1))
            dW[ssl].bump(16)
            if item < 8:
                which, q = item // 4, item % 4
                src = gp_w if which == 0 else up_w
                E("sync", lambda e, s=src, ei=ei, q=q, ssl=ssl:
                  e.dma_start(out=s_stage[:, ssl * 1024:(ssl + 1) * 1024],
                              in_=s[ei, :, q * 1024:(q + 1) * 1024]).then_inc(dW[ssl].h, 16))
                W("scalar", dW[ssl], dW[ssl].n)
                sFr.bump()
                E("scalar", lambda e, d=dsts[which][slot], q=q, ssl=ssl:
                  e.copy(out=d[:, q * 1024:(q + 1) * 1024],
                         in_=s_stage[:, ssl * 1024:(ssl + 1) * 1024]).then_inc(sFr.h, 1))
            else:
                q = item - 8
                E("sync", lambda e, ei=ei, q=q, ssl=ssl:
                  e.dma_start(out=s_stage[:, ssl * 1024:(ssl + 1) * 1024],
                              in_=dn_w[ei, :, q * 1024:(q + 1) * 1024]).then_inc(dW[ssl].h, 16))
                W("scalar", dW[ssl], dW[ssl].n)
                W("scalar", dDN, (ei * 4 + q) * 16)
                sFr.bump()
                E("scalar", lambda e, q=q, ssl=ssl:
                  e.copy(out=b_dnc[:, :],
                         in_=s_stage[:, ssl * 1024:(ssl + 1) * 1024]).then_inc(sFr.h, 1))
                W("sync", sFr, sFr.n)
                dDN.bump(16)
                E("sync", lambda e, ei=ei, q=q:
                  e.dma_start(out=dn_dram[ei, :, q * 1024:(q + 1) * 1024],
                              in_=b_dnc[:, :]).then_inc(dDN.h, 16))
            return sFr.n

        def emit_wload_chunk(ei, phase):
            wc = sFr.n
            for item in range(phase * 3, phase * 3 + 3):
                wc = emit_wload_item(ei, item)
            return wc

        part2_v = [0]
        upe = [0, 0]

        def emit_expert_part1(ei, sc, wcast):
            slot = ei % 2
            gp_t = b_gp0 if slot == 0 else b_gp1
            up_t = b_up0 if slot == 0 else b_up1
            W("tensor", sFr, wcast)
            W("tensor", vG, crep_free[0])
            W("tensor", sM, ROUTER_S)
            tM.bump()
            E("tensor", lambda e, ei=ei, sc=sc:
              e.matmul(out=FB(4), lhsT=b_sel16[:, ei * 128:(ei + 1) * 128],
                       rhs=cT[:, sc * TC:(sc + 1) * TC], start=True, stop=True).then_inc(tM.h, 1))
            CR = tM.n
            for it in range(4):
                gb = it % 2
                ub = 2 + it % 2
                W("tensor", sM, guf_g[gb])
                W("tensor", sM, upe[ub - 2])
                for dc in range(NDC):
                    tM.bump()
                    E("tensor", lambda e, sc=sc, it=it, dc=dc, gb=gb, w=gp_t:
                      e.matmul(out=FB(gb),
                               lhsT=w[:, dc * II + it * 128: dc * II + (it + 1) * 128],
                               rhs=xn[:, dc * TP3 + 4 + sc * TC: dc * TP3 + 4 + (sc + 1) * TC],
                               start=(dc == 0), stop=(dc == NDC - 1)).then_inc(tM.h, 1))
                GD = tM.n
                for dc in range(NDC):
                    tM.bump()
                    E("tensor", lambda e, sc=sc, it=it, dc=dc, ub=ub, w=up_t:
                      e.matmul(out=FB(ub),
                               lhsT=w[:, dc * II + it * 128: dc * II + (it + 1) * 128],
                               rhs=xn[:, dc * TP3 + 4 + sc * TC: dc * TP3 + 4 + (sc + 1) * TC],
                               start=(dc == 0), stop=(dc == NDC - 1)).then_inc(tM.h, 1))
                UD = tM.n
                W("scalar", tM, GD)
                if it == 0:
                    if ei == 0 and sc == 0:
                        W("scalar", dSpX, ALL_SPX)
                    W("scalar", vM, part2_v[0])
                sM.bump()
                E("scalar", lambda e, gb=gb, it=it:
                  e.activation(out=b_xc[:, it * TC:(it + 1) * TC], in_=FB(gb),
                               func=AF.Silu).then_inc(sM.h, 1))
                guf_g[gb] = sM.n
                W("scalar", tM, UD)
                sM.bump()
                E("scalar", lambda e, ub=ub, it=it:
                  e.copy(out=b_xc[:, (4 + it) * TC:(5 + it) * TC],
                         in_=FB(ub)).then_inc(sM.h, 1))
                upe[ub - 2] = sM.n
            return CR, sM.n

        def emit_expert_part2(ei, sc, st):
            CR, SEV = st
            for it in range(4):
                W("vector", sM, SEV)
                W("vector", tM, CR)
                if ei * 4 + sc >= 1 and it == 0:
                    W("vector", dA, (ei * 4 + sc) * 16)
                E("vector", lambda e, it=it:
                  e.tensor_mul(s_m1[:, :],
                               b_xc[:, it * TC:(it + 1) * TC],
                               b_xc[:, (4 + it) * TC:(5 + it) * TC]))
                vM.bump()
                E("vector", lambda e, it=it:
                  e.tensor_tensor(out=b_act[:, it * TC:(it + 1) * TC],
                                  in0=s_m1[:, :],
                                  in1=FB(4), op=OP.mult).then_inc(vM.h, 1))
            part2_v[0] = vM.n
            vG.bump()
            E("vector", lambda e: e.engine_nop().then_inc(vG.h, 1))
            crep_free[0] = vG.n
            act_v = vM.n
            dA.bump(16)

            def _act_spill(ei=ei, sc=sc, act_v=act_v):
                W("sync", vM, act_v)
                E("sync", lambda e, ei=ei, sc=sc:
                  e.dma_start(out=act_dram[ei, sc, :, :],
                              in_=b_act[:, :]).then_inc(dA.h, 16))
            pend_sync.append(_act_spill)

        TM_PROLOGUE = [tM.n]
        DBG_U1 = [True] if dbg else [False]
        DBG_K9 = [True]

        mix_units = [("p", tc) for tc in range(NTC)] + [("m", tc) for tc in range(NTC)]

        def mixer_gen():
            for kind, tc in mix_units:
                if dbg and kind == "m" and tc == 0:
                    W("sync", vScan, abk[0])
                    dOut.bump(16)
                    E("sync", lambda e: e.dma_start(out=dbg_m2[:, 3840:3968],
                                                    in_=s_carry[:, :]).then_inc(dOut.h, 16))
                    W("vector", dOut, dOut.n)
                bs = emit_bc(tc, kind == "p")
                for dc in range(NDC):
                    g = emit_mixer_dc(tc, dc, kind == "p", bs)
                    next(g)
                    yield
                    for _ in g:
                        pass
                    yield

        def expert_gen():
            wc = 0
            for ph in range(4):
                wc = emit_wload_chunk(0, ph)
            wc_next = 0
            for es in range(NE):
                for sc in range(4):
                    if es + 1 < NE:
                        wc_next = emit_wload_chunk(es + 1, sc)
                    st = emit_expert_part1(es, sc, wc)
                    yield
                    emit_expert_part2(es, sc, st)
                    yield
                wc = wc_next

        mg, eg = mixer_gen(), expert_gen()
        done_m = done_e = False
        while not (done_m and done_e):
            if not done_m:
                try:
                    next(mg)
                except StopIteration:
                    done_m = True
            if not done_e:
                try:
                    next(eg)
                except StopIteration:
                    done_e = True
        flush_pend()

        ALL_ACT = dA.n
        ALL_DN = dDN.n
        ALL_ATT = dAT.n
        PHASEA_T = tM.n
        W("sync", dA, ALL_ACT)
        W("sync", dDN, ALL_DN)
        W("tensor", sM, sM.n)
        W("tensor", vM, vM.n)
        phaseb_free = [0]
        lastw = {}
        asm_done = {}
        asm_fev = {}
        asm_out = {}
        for sc in range(4):
            for ei in range(NE):
                asl = ei % 2
                if ei >= 2:
                    W("sync", tM, lastw[(sc, ei - 2)])
                elif sc > 0:
                    W("sync", tM, lastw[(sc - 1, NE - 2 + ei)])
                else:
                    W("sync", tM, PHASEA_T)
                dAI[asl].bump(16)
                E("sync", lambda e, ei=ei, sc=sc, asl=asl:
                  e.dma_start(out=b_actin[:, asl * 4 * TC:(asl + 1) * 4 * TC],
                              in_=act_dram[ei, sc, :, :]).then_inc(dAI[asl].h, 16))
                AIN = dAI[asl].n
                dDNI[asl].bump(16)
                E("sync", lambda e, ei=ei, asl=asl:
                  e.dma_start(out=b_dnin[:, asl * 4096:(asl + 1) * 4096],
                              in_=dn_dram[ei, :, :]).then_inc(dDNI[asl].h, 16))
                DIN = dDNI[asl].n
                W("tensor", dAI[asl], AIN)
                W("tensor", dDNI[asl], DIN)
                if ei == 0:
                    W("tensor", sM, phaseb_free[0])
                for ic in range(4):
                    for ht in range(NDC):
                        oap = FB(ht)
                        tM.bump()
                        E("tensor", lambda e, ei=ei, ic=ic, ht=ht, asl=asl, o=oap:
                          e.matmul(out=o,
                                   lhsT=b_dnin[:, asl * 4096 + ic * 1024 + ht * 128:
                                               asl * 4096 + ic * 1024 + (ht + 1) * 128],
                                   rhs=b_actin[:, (asl * 4 + ic) * TC:(asl * 4 + ic + 1) * TC],
                                   start=(ei == 0 and ic == 0),
                                   stop=(ei == NE - 1 and ic == 3)).then_inc(tM.h, 1))
                lastw[(sc, ei)] = tM.n
            DOWN = tM.n
            W("sync", dAT, ALL_ATT)
            for ht in range(NDC):
                k = sc * NDC + ht
                sl = k % 2
                if k >= 2:
                    W("sync", vM, asm_done.get(k - 2, 0))
                dATI[sl].bump(32)
                E("sync", lambda e, ht=ht, sc=sc, sl=sl:
                  e.dma_start(out=s_ati[:, sl * TC:(sl + 1) * TC],
                              in_=attn_dram[ht, :, sc * TC:(sc + 1) * TC]).then_inc(dATI[sl].h, 16))
                E("sync", lambda e, ht=ht, sc=sc, sl=sl:
                  e.dma_start(out=s_xi[:, sl * TC:(sl + 1) * TC],
                              in_=x_fm[ht * 128:(ht + 1) * 128, sc * TC:(sc + 1) * TC]).then_inc(dATI[sl].h, 16))
                ATI = dATI[sl].n
                W("scalar", tM, DOWN)
                W("scalar", vM, asm_fev.get(k - 2, 0))
                oap = FB(ht)
                sM.bump()
                E("scalar", lambda e, ht=ht, sl=sl, o=oap:
                  e.copy(out=s_fev[:, sl * TC:(sl + 1) * TC], in_=o).then_inc(sM.h, 1))
                FEV = sM.n
                W("vector", sM, FEV)
                W("vector", dATI[sl], ATI)
                W("vector", dOut, asm_out.get(k - 2, 0))
                E("vector", lambda e, sl=sl:
                  e.tensor_add(s_o1[:, sl * TC:(sl + 1) * TC], s_fev[:, sl * TC:(sl + 1) * TC],
                               s_ati[:, sl * TC:(sl + 1) * TC]))
                vM.bump()
                E("vector", lambda e, sl=sl:
                  e.tensor_add(s_o1[:, sl * TC:(sl + 1) * TC], s_o1[:, sl * TC:(sl + 1) * TC],
                               s_xi[:, sl * TC:(sl + 1) * TC]).then_inc(vM.h, 1))
                OA = vM.n
                asm_done[k] = OA
                asm_fev[k] = OA
                W("sync", vM, OA)
                dOut.bump(16)
                asm_out[k] = dOut.n
                E("sync", lambda e, ht=ht, sc=sc, sl=sl:
                  e.dma_start(out=y_out[ht * 128:(ht + 1) * 128, sc * TC:(sc + 1) * TC],
                              in_=s_o1[:, sl * TC:(sl + 1) * TC]).then_inc(dOut.h, 16))
            phaseb_free[0] = sM.n

        W("sync", dOut, dOut.n)
        if dbg:
            dbv = [0]

            def dump(dst, srcap):
                dbv[0] += 16
                E("sync", lambda e, d=dst, s=srcap: e.dma_start(out=d, in_=s).then_inc(dOut.h, 16))
                dOut.n += 16
            dump(dbg_xn, xn[:, :])
            dump(dbg_ct, cT[:, :])
            dump(dbg_bb, b_B[:, :])
            dump(dbg_cc, b_C[:, :])
            dump(dbg_misc[:, 0:128], s_carry[:, :])
            dump(dbg_misc[:, 128:144], s_stat[:, :])
            dump(dbg_misc[:, 144:272], s_app[:, :])
            dump(dbg_misc[:, 272:400], s_papp[:, :])
            dump(dbg_misc[:, 400:912], s_y[:, :])
            dump(dbg_misc[:, 912:1424], s_y[:, :])
            dump(dbg_misc[:, 1424:1936], s_abar[:, 0:512])
            dump(dbg_misc[:, 1936:2448], s_wr[:, 0:512])
            dump(dbg_misc[:, 2448:2960], s_u[:, 0:512])
            dump(dbg_misc[:, 2960:3472], s_dl[:, 0:512])
            W("sync", dOut, dOut.n)

        with nc.Block() as block:
            @block.sync
            def _(e):
                for f in ops["sync"]:
                    f(e)

            @block.tensor
            def _(e):
                for f in ops["tensor"]:
                    f(e)

            @block.vector
            def _(e):
                for f in ops["vector"]:
                    f(e)

            @block.scalar
            def _(e):
                for f in ops["scalar"]:
                    f(e)

            @block.gpsimd
            def _(e):
                for f in ops["gpsimd"]:
                    f(e)

    return nc


_NC_CACHE = {}


def _packs(inputs):
    f = np.float32
    conv_w = np.asarray(inputs["L0_attn_conv_w"], f)
    conv_b = np.asarray(inputs["L0_attn_conv_b"], f)
    xproj = np.asarray(inputs["L0_attn_xproj_w"], f)
    dt_w = np.asarray(inputs["L0_attn_dt_w"], f)
    dt_b = np.asarray(inputs["L0_attn_dt_b"], f)
    a_log = np.asarray(inputs["L0_attn_a_log"], f)
    Dp = np.asarray(inputs["L0_attn_Dp"], f)
    gw = np.asarray(inputs["L0_ffn_gate_weight"], f)
    e_bias = np.asarray(inputs["L0_ffn_e_bias"], f)
    gp = np.asarray(inputs["L0_ffn_gate_proj"], f)
    up = np.asarray(inputs["L0_ffn_up_proj"], f)
    dn = np.asarray(inputs["L0_ffn_down_proj"], f)

    def fm_pack(v):
        return np.ascontiguousarray(v.reshape(NDC, 128).T)

    d = {}
    d["w33"] = np.ascontiguousarray(
        xproj.T.reshape(NDC, 128, 33).transpose(1, 0, 2).reshape(128, NDC * 33))
    d["gwp"] = np.ascontiguousarray(
        gw.T.reshape(NDC, 128, 16).transpose(1, 0, 2).reshape(128, NDC * 16))
    d["convw"] = np.ascontiguousarray(
        conv_w[:, 0, :].reshape(NDC, 128, 4).transpose(1, 2, 0).reshape(128, 4 * NDC))
    d["cb_pp"] = fm_pack(conv_b)
    d["dtw_rep"] = np.ascontiguousarray(np.tile(dt_w[:, 0][None, :], (128, 1)))
    d["dtb_rep"] = np.ascontiguousarray(np.tile(dt_b[None, :], (128, 1)))
    d["alog"] = np.ascontiguousarray(
        a_log.reshape(NDC, 128, N).transpose(1, 0, 2).reshape(128, NDC * N))
    d["dp_pp"] = fm_pack(Dp)
    d["ebias"] = np.ascontiguousarray(np.tile(e_bias[None, :], (128, 1)))
    d["ebr"] = np.ascontiguousarray(np.tile(e_bias[None, :], (128, NTT)))
    d["sel33"] = np.ascontiguousarray(np.repeat(np.eye(33, dtype=f), 128, axis=1))
    d["sel16"] = np.ascontiguousarray(np.repeat(np.eye(16, dtype=f), 128, axis=1))
    d["iden"] = np.eye(128, dtype=f)
    cst = np.zeros((128, 8), f)
    cst[:, C_NEG1] = -1.0
    cst[:, C_ONE] = 1.0
    cst[:, C_EPS] = 1e-5
    cst[:, C_HALF] = 0.5
    d["csts"] = cst
    d["gp_w"] = np.ascontiguousarray(
        gp.transpose(0, 2, 1).reshape(NE, NDC, 128, II).transpose(0, 2, 1, 3).reshape(NE, 128, NDC * II))
    d["up_w"] = np.ascontiguousarray(
        up.transpose(0, 2, 1).reshape(NE, NDC, 128, II).transpose(0, 2, 1, 3).reshape(NE, 128, NDC * II))
    d["dn_w"] = np.ascontiguousarray(
        dn.transpose(0, 2, 1).reshape(NE, 4, 128, D).transpose(0, 2, 1, 3).reshape(NE, 128, 4 * D))
    d["p_w33"] = d["w33"]
    d["p_convw"] = d["convw"]
    d["p_cb_pp"] = d["cb_pp"]
    d["p_dtw_rep"] = d["dtw_rep"]
    d["p_dtb_rep"] = d["dtb_rep"]
    d["p_alog"] = d["alog"]
    return d


def kernel(**inputs):
    inputs = dict(inputs)
    x = np.asarray(inputs["L0_attn_x"], np.float32)
    shared = _packs(inputs)
    zero_like = {k: np.zeros_like(v) for k, v in shared.items()
                 if k.startswith("p_")}

    in_maps = []
    for c in range(8):
        b, hf = c // 2, c % 2
        m = dict(shared)
        xp = np.ascontiguousarray(x[b, 0:T])
        if hf == 0:
            m.update(zero_like)
        m["hgate"] = np.full((128, 1), float(hf), np.float32)
        m["xp_tm"] = xp
        m["x_tm"] = np.ascontiguousarray(x[b, hf * T:(hf + 1) * T])
        m["x_fm"] = np.ascontiguousarray(x[b, hf * T:(hf + 1) * T].T)
        in_maps.append(m)

    dbg = bool(inputs.get("_dbg"))
    key = "nc_dbg" if dbg else "nc"
    if key not in _NC_CACHE:
        _NC_CACHE[key] = build_nc(dbg)
    nc = _NC_CACHE[key]
    res = run_bass_kernel_spmd(nc, in_maps, core_ids=list(range(8)))

    out = np.empty((B, L, D), np.float32)
    for c in range(8):
        b, hf = c // 2, c % 2
        out[b, hf * T:(hf + 1) * T, :] = np.asarray(res.results[c]["y_fm"]).T
    if dbg:
        return out, res.results
    return out
